# version 07
# baseline (speedup 1.0000x reference)
import numpy as np
from contextlib import ExitStack
import concourse.bass as bass
import concourse.mybir as mybir
from concourse.bass_utils import run_bass_kernel_spmd

F32 = mybir.dt.float32
BF16 = mybir.dt.bfloat16
AF = mybir.ActivationFunctionType
ALU = mybir.AluOpType
AXX = mybir.AxisListType.X

HID = 4096
T = 1024
XROWS = 1152
NK = 32
INTER = 8192
NPROJ = 18560
NZ = 8
PD = 64
RUN = 2048
RUNP = 2052
IP_BLK = [384] * 48 + [128]
EPS = 1e-5

_PHASES = 6
_DEBUG = False


def _runs(a):
    a = np.ascontiguousarray(a, dtype=np.float32)
    M = a.shape[1]
    nr = (M + RUN - 1) // RUN
    out = np.zeros((128, nr, RUNP), np.float32)
    pad = np.zeros((128, nr * RUN), np.float32)
    pad[:, :M] = a
    out[:, :, :RUN] = pad.reshape(128, nr, RUN)
    return out


def _stage_weights(ins):
    st = {}
    w = np.asarray(ins["L0_attn_in_proj_w"], np.float32)
    arr = w.reshape(NPROJ, NK, 128).transpose(2, 1, 0)
    blobs, ip_off, off = [], [], 0
    for b, cb in enumerate(IP_BLK):
        chb = sum(IP_BLK[:b])
        sub = np.ascontiguousarray(arr[:, :, chb:chb + cb]).reshape(128, NK * cb)
        r = _runs(sub)
        ip_off.append((off, r.shape[1]))
        off += r.shape[1] * RUNP
        blobs.append(r.reshape(128, -1))
    st["w_ip"] = np.concatenate(blobs, axis=1)
    st["ip_off"] = ip_off

    wb = np.asarray(ins["L0_ffn_W_base"], np.float32)
    arr = wb.reshape(HID, NK, 128).transpose(2, 1, 0)
    blobs = []
    for ob in range(16):
        sub = np.ascontiguousarray(arr[:, :, ob * 256:(ob + 1) * 256]).reshape(128, NK * 256)
        blobs.append(_runs(sub).reshape(128, -1))
    st["w_fb"] = np.concatenate(blobs, axis=1)

    wo = np.asarray(ins["L0_attn_out_proj_w"], np.float32)
    arr = wo.reshape(HID, 64, 128).transpose(2, 1, 0)
    blobs = []
    for ob in range(32):
        sub = np.ascontiguousarray(arr[:, :, ob * 128:(ob + 1) * 128]).reshape(128, 64 * 128)
        blobs.append(_runs(sub).reshape(128, -1))
    st["w_wo"] = np.concatenate(blobs, axis=1)

    A = np.asarray(ins["L0_ffn_A"], np.float32).reshape(128, HID)
    arr = np.ascontiguousarray(A.T).reshape(NK, 128, 128).transpose(1, 0, 2)
    st["w_al"] = _runs(np.ascontiguousarray(arr).reshape(128, NK * 128)).reshape(128, -1)

    Bm = np.asarray(ins["L0_ffn_B"], np.float32)
    st["w_bl"] = _runs(np.transpose(Bm, (0, 2, 1)).reshape(128, HID)).reshape(128, -1)

    wr = np.asarray(ins["L0_ffn_W_router"], np.float32)
    arr = np.ascontiguousarray(wr.T).reshape(NK, 128, 8).transpose(1, 0, 2)
    st["w_rt"] = _runs(np.ascontiguousarray(arr).reshape(128, NK * 8)).reshape(128, -1)

    cw = np.asarray(ins["L0_attn_conv_w"], np.float32).reshape(10240, 4)
    st["c_convw"] = np.ascontiguousarray(cw.reshape(80, 128, 4).transpose(1, 0, 2).reshape(128, 320))
    st["c_convb"] = np.ascontiguousarray(np.asarray(ins["L0_attn_conv_b"], np.float32).reshape(80, 128).T)
    st["c_dtb"] = np.asarray(ins["L0_attn_dt_bias"], np.float32).reshape(128, 1)
    st["c_alog"] = np.asarray(ins["L0_attn_A_log"], np.float32).reshape(128, 1)
    D = np.asarray(ins["L0_attn_D"], np.float32)
    st["c_dcol"] = np.ascontiguousarray(D[(np.arange(INTER) // PD)].reshape(64, 128).T)
    st["c_nw"] = np.ascontiguousarray(np.asarray(ins["L0_attn_norm_w"], np.float32).reshape(64, 128).T)
    bb = np.zeros((1, 2, RUNP), np.float32)
    bb[0, :, :RUN] = np.asarray(ins["L0_ffn_b_base"], np.float32).reshape(2, RUN)
    st["c_bb"] = bb.reshape(1, 2 * RUNP)
    st["c_id"] = np.eye(128, dtype=np.float32)
    st["c_tri"] = np.triu(np.ones((128, 128), np.float32))
    st["c_on3"] = np.ones((3, 128), np.float32)
    st["c_on1"] = np.ones((1, 512), np.float32)
    sc = np.zeros((128, 2), np.float32)
    sc[:, 0] = 1e-5
    sc[:, 1] = 1.0
    st["c_sc"] = sc
    em = np.zeros((1, 1024), np.float32)
    for e in range(8):
        em[0, e * 128 + e * 16:e * 128 + (e + 1) * 16] = 1.0
    st["c_em"] = em
    return st


class Ctx:
    def __init__(self, nc):
        self.nc = nc
        self.sem = {}
        self.cnt = {}

    def mksem(self, name):
        self.sem[name] = self.nc.alloc_semaphore(name=name)
        self.cnt[name] = 0

    def dma(self, q, out, in_, key, **kw):
        k = "D" + key
        if k not in self.sem:
            self.mksem(k)
        q.dma_start(out=out, in_=in_, **kw).then_inc(self.sem[k], 16)
        self.cnt[k] += 16
        return (k, self.cnt[k])

    def wait2(self, eng, tok):
        if tok is None:
            return
        k, v = tok
        if v > 0:
            eng.wait_ge(self.sem[k], v)

    def final_dma_waits(self, eng):
        for k in sorted(self.sem):
            if k.startswith("D") and self.cnt[k] > 0:
                eng.wait_ge(self.sem[k], self.cnt[k])

    def inc(self, inst, sem):
        inst.then_inc(self.sem[sem], 1)
        self.cnt[sem] += 1
        return self.cnt[sem]

    def wait(self, eng, sem, tok):
        if tok > 0:
            eng.wait_ge(self.sem[sem], tok)

    def fence(self, eng, sem):
        tok = self.inc(eng.drain(), sem)
        eng.wait_ge(self.sem[sem], tok)


def build_nc(debug=False, phases=6):
    nc = bass.Bass("TRN2", target_bir_lowering=False, detect_race_conditions=False)

    def DI(name, shape, dt=F32):
        return nc.dram_tensor(name, list(shape), dt, kind="ExternalInput").ap()

    ipw_cols = sum(((NK * cb + RUN - 1) // RUN) * RUNP for cb in IP_BLK)
    xh = DI("xh", (XROWS, HID))
    xT = DI("xT", (HID, T))
    w_ip = DI("w_ip", (128, ipw_cols))
    w_fb = DI("w_fb", (128, 16 * 4 * RUNP))
    w_wo = DI("w_wo", (128, 32 * 4 * RUNP))
    w_al = DI("w_al", (128, 2 * RUNP))
    w_bl = DI("w_bl", (128, 2 * RUNP))
    w_rt = DI("w_rt", (128, 1 * RUNP))
    c_convw = DI("c_convw", (128, 320))
    c_convb = DI("c_convb", (128, 80))
    c_dtb = DI("c_dtb", (128, 1))
    c_alog = DI("c_alog", (128, 1))
    c_dcol = DI("c_dcol", (128, 64))
    c_nw = DI("c_nw", (128, 64))
    c_bb = DI("c_bb", (1, 2 * RUNP))
    c_id = DI("c_id", (128, 128))
    c_tri = DI("c_tri", (128, 128))
    c_on3 = DI("c_on3", (3, 128))
    c_on1 = DI("c_on1", (1, 512))
    c_em = DI("c_em", (1, 1024))
    c_sc = DI("c_sc", (128, 2))
    c_flag = DI("c_flag", (1, 1))

    outT = nc.dram_tensor("outT", [HID, T], F32, kind="ExternalOutput").ap()

    d_hsbc = nc.dram_tensor("d_hsbc", [80, 128, 1032], BF16, kind="Internal").ap()
    d_sg = nc.dram_tensor("d_sg", [NZ, 128, 8224], BF16, kind="Internal").ap()
    d_xch = nc.dram_tensor("d_xch", [NZ, 128, INTER], BF16, kind="Internal").ap()
    d_s = nc.dram_tensor("d_s", [NZ, 128, INTER], BF16, kind="Internal").ap()
    d_gffn = nc.dram_tensor("d_gffn", [HID, 1032], BF16, kind="Internal").ap()
    d_ggt = nc.dram_tensor("d_ggt", [64, 128, 1032], BF16, kind="Internal").ap()
    d_comb = nc.dram_tensor("d_comb", [8, 1024], BF16, kind="Internal").ap()
    d_ccin = nc.dram_tensor("d_ccin", [128, INTER], BF16, kind="Internal").ap()
    d_ccout = nc.dram_tensor("d_ccout", [256, INTER], BF16, kind="Internal").ap()

    dbg = {}
    if debug:
        def DO(name, shape, dt=F32):
            return nc.dram_tensor(name, list(shape), dt, kind="ExternalOutput").ap()
        dbg["xnt"] = DO("dbg_xnt", (128, 32 * 1028))
        dbg["hsbc"] = DO("dbg_hsbc", (80, 128, 1032), BF16)
        dbg["sg"] = DO("dbg_sg", (NZ, 128, 8224), BF16)
        dbg["dt"] = DO("dbg_dt", (128, 1024))
        dbg["acum"] = DO("dbg_acum", (128, 1024))
        dbg["xch"] = DO("dbg_xch", (NZ, 128, INTER), BF16)
        dbg["s"] = DO("dbg_s", (NZ, 128, INTER), BF16)
        dbg["prev"] = DO("dbg_prev", (128, INTER))
        dbg["gffn"] = DO("dbg_gffn", (HID, 1032), BF16)
        dbg["ggt"] = DO("dbg_ggt", (64, 128, 1032), BF16)
        dbg["yz"] = DO("dbg_yz", (128, INTER))
        dbg["gtm"] = DO("dbg_gtm", (128, 1024))
        dbg["ylast"] = DO("dbg_ylast", (128, INTER), BF16)
        dbg["expa"] = DO("dbg_expa", (128, 128))

    es = ExitStack()

    def SB(name, shape, dtype):
        return es.enter_context(nc.sbuf_tensor(name, list(shape), dtype)).ap()

    ar1 = nc.alloc_sbuf_tensor("arena1", [128, 131072], mybir.dt.uint8)
    ar2 = nc.alloc_sbuf_tensor("arena2", [128, 49152], mybir.dt.uint8)
    a1 = nc.lookup_mloc(ar1).addr
    a2 = nc.lookup_mloc(ar2).addr

    def carve(base, name, shape, dtype, off):
        h = nc.virtual_sbuf_tensor(name, list(shape), dtype)
        nc._tensor(h.name, list(shape), dtype, type="SB")
        ml = nc.lookup_mloc(h)
        ml.allocated = True
        ml.addr = base + off
        return h.ap()

    xnT = carve(a1, "xnT", (128, 32 * 1028), BF16, 0)
    xzb = carve(a1, "xzb", (128, INTER), BF16, 65792)
    szb = carve(a1, "szb", (128, INTER), BF16, 82176)
    hszh = carve(a1, "hszh", (128, 4096), BF16, 98560)
    bzb = carve(a1, "bzb", (128, 1024), BF16, 106752)
    btm = carve(a1, "btm", (128, 256), BF16, 108800)
    xch2 = carve(a1, "xch2", (128, 1024), BF16, 109312)
    sgbuf = carve(a1, "sgbuf", (128, 2048), BF16, 109824)
    sgTs = carve(a1, "sgTs", (128, 2048), BF16, 113920)
    hbuf = carve(a1, "hbuf", (128, 2048), BF16, 118016)
    cbuf = carve(a1, "cbuf", (128, 1024), F32, 122112)
    sqdump = carve(a1, "sqdump", (128, HID), BF16, 122112)
    bfl = carve(a1, "bfl", (128, HID), BF16, 65792)
    hwT = carve(a1, "hwT", (128, 1024), BF16, 82176)
    sg_z = carve(a1, "sg_z", (128, INTER), BF16, 0)
    xch_z = carve(a1, "xch_z", (128, INTER), BF16, 16384)
    s_z = carve(a1, "s_z", (128, INTER), BF16, 32768)
    hs_z = carve(a1, "hs_z", (128, INTER), BF16, 49152)
    bz_c = carve(a1, "bz_c", (128, 1024), BF16, 65536)
    cz_c = carve(a1, "cz_c", (128, 1024), BF16, 67584)
    y_z = carve(a1, "y_z", (128, INTER), BF16, 69632)
    pvb = carve(a1, "pvb", (128, 512), BF16, 86016)
    xinm = carve(a1, "xinm", (128, INTER), BF16, 102912)
    prevC = carve(a1, "prevC", (128, INTER), F32, 87040)
    ggTs = carve(a1, "ggTs", (128, INTER), BF16, 32768)
    csc = carve(a1, "csc", (128, 2048), BF16, 123392)
    ggT = carve(a1, "ggT", (128, 65536), BF16, 0)

    xa0 = carve(a2, "xa0", (128, HID), F32, 0)
    xa1 = carve(a2, "xa1", (128, HID), F32, 16384)
    xnb = carve(a2, "xnb", (128, HID), BF16, 32768)
    wsb0 = carve(a2, "wsb0", (128, 12288), BF16, 0)
    wsb1 = carve(a2, "wsb1", (128, 12288), BF16, 24576)
    prev = carve(a2, "prev", (128, INTER), F32, 0)
    acT = carve(a2, "acT", (128, 128), F32, 32768)
    spl3 = carve(a2, "spl3", (128, 384), BF16, 33280)
    splt = carve(a2, "splt", (128, 256), F32, 34048)
    r3 = carve(a2, "r3", (3, 128), BF16, 35072)
    az3 = carve(a2, "az3", (128, 4), BF16, 35328)
    eazb = carve(a2, "eazb", (128, 128), F32, 35360)
    dec_zT = carve(a2, "dec_zT", (128, 128), F32, 35872)
    ddz = carve(a2, "ddz", (128, 128), F32, 36384)
    wrk = carve(a2, "wrk", (128, 128), F32, 36896)
    wrk2 = carve(a2, "wrk2", (128, 128), F32, 37408)
    gtm = carve(a2, "gtm", (128, 1024), F32, 37920)
    expA = carve(a2, "expA", (128, 128), F32, 42016)
    ltw = carve(a2, "ltw", (128, 128), F32, 42528)
    splTI = carve(a2, "splTI", (128, 384), BF16, 43040)
    azcol = carve(a2, "azcol", (128, 2), F32, 43808)
    ltw8 = carve(a2, "ltw8", (128, 1024), F32, 0)
    mtb8 = carve(a2, "mtb8", (128, 1024), BF16, 4096)
    wrk8 = carve(a2, "wrk8", (128, 512), F32, 6144)
    r3b8 = carve(a2, "r3b8", (3, 1024), BF16, 8192)
    wfb0 = carve(a2, "wfb0", (128, 8192), BF16, 0)
    wfb1 = carve(a2, "wfb1", (128, 8192), BF16, 16384)
    crep = carve(a2, "crep", (128, 1024), F32, 44352)
    combrow = carve(a2, "combrow", (1, 1024), BF16, 32768)
    bbasebf = carve(a2, "bbasebf", (1, HID), BF16, 34816)

    wdb0 = carve(a2, "wdb0", (128, 8192), BF16, 0)
    wdb1 = carve(a2, "wdb1", (128, 8192), BF16, 16384)
    otb = carve(a2, "otb", (128, 1024), F32, 32768)
    gft = carve(a2, "gft", (128, 1024), BF16, 36864)
    xtt = carve(a2, "xtt", (128, 1024), F32, 38912)

    dtTb = SB("dtTb", (128, 1024), F32)
    acum = SB("acum", (128, 1024), F32)
    dtcm = SB("dtcm", (128, 1024), F32)
    acolh = SB("acolh", (128, 1), F32)
    k_id = SB("k_id", (128, 128), BF16)
    k_tri = SB("k_tri", (128, 128), F32)
    k_on3 = SB("k_on3", (3, 128), BF16)
    k_on1 = SB("k_on1", (1, 512), BF16)
    k_em = SB("k_em", (1, 1024), BF16)
    k_convw = SB("k_convw", (128, 320), F32)
    k_convb = SB("k_convb", (128, 80), F32)
    k_dtb = SB("k_dtb", (128, 1), F32)
    k_alog = SB("k_alog", (128, 1), F32)
    k_dcol = SB("k_dcol", (128, 64), F32)
    k_nw = SB("k_nw", (128, 64), F32)
    k_flag = SB("k_flag", (1, 1), F32)
    k_sc = SB("k_sc", (128, 2), F32)
    k_flagb = SB("k_flagb", (1, 1), BF16)
    flagc = SB("flagc", (128, 1), F32)
    lncol = SB("lncol", (128, 4), F32)
    tpw = SB("tpw", (128, 64), F32)
    tpc = SB("tpc", (128, 16), F32)
    comb = SB("comb", (128, 64), F32)
    combb = SB("combb", (128, 64), BF16)
    dbuf = SB("dbuf", (128, 1024), F32)
    tbuf = SB("tbuf", (128, 1024), F32)
    sumg = SB("sumg", (128, 8), F32)
    rstd = SB("rstd", (128, 8), F32)
    eaz1 = SB("eaz1", (1, 128), F32)
    eaz2 = SB("eaz2", (1, 256), BF16)

    pp0 = es.enter_context(nc.psum_tensor("pp0", [128, 1536], F32))
    pp1 = es.enter_context(nc.psum_tensor("pp1", [128, 1536], F32))
    pbA = es.enter_context(nc.psum_tensor("pbA", [128, 1024], BF16))
    pbB = es.enter_context(nc.psum_tensor("pbB", [128, 1024], BF16))
    P0, P1, QA, QB = pp0.ap(), pp1.ap(), pbA.ap(), pbB.ap()
    BK = [P0[:, 0:512], P0[:, 512:1024], P0[:, 1024:1536],
          P1[:, 0:512], P1[:, 512:1024], P1[:, 1024:1536]]

    c = Ctx(nc)
    for s in ("LD", "WD", "PE", "VE", "SE", "CC", "ST"):
        c.mksem(s)
    V, S_, TE, SY, GP = nc.vector, nc.scalar, nc.tensor, nc.sync, nc.gpsimd

    blk = es.enter_context(nc.Block())

    c.dma(SY, k_tri[:], c_tri, "KC")
    c.dma(SY, k_convw[:], c_convw, "KC")
    c.dma(SY, k_convb[:], c_convb, "KC")
    c.dma(SY, k_dtb[:], c_dtb, "KC")
    c.dma(SY, k_alog[:], c_alog, "KC")
    c.dma(SY, k_dcol[:], c_dcol, "KC")
    c.dma(SY, k_nw[:], c_nw, "KC")
    c.dma(SY, k_flag[:], c_flag, "KC")
    t_fl = c.dma(SY, k_sc[:], c_sc, "KC")
    c.dma(GP, k_id[:], c_id, "KW")
    c.dma(GP, k_on3[:], c_on3, "KW")
    c.dma(GP, k_on1[:], c_on1, "KW")
    t_km = c.dma(GP, k_em[:], c_em, "KW")

    c.wait2(S_, t_fl)
    S_.activation(acolh[:], k_alog[:], AF.Exp)
    t0 = c.inc(S_.drain(), "SE")
    c.wait(V, "SE", t0)
    V.tensor_scalar_mul(acolh[:], acolh[:], -1.0)
    t_acol = c.inc(V.drain(), "VE")
    c.wait2(V, t_fl)
    V.tensor_copy(k_flagb[:], k_flag[:])
    t_flb = c.inc(V.drain(), "VE")
    c.wait2(TE, t_km)
    c.wait(TE, "VE", t_flb)
    TE.matmul(BK[0][:, 0:1], k_on1[0:1, 0:128], k_flagb[:], start=True, stop=True)
    t1 = c.inc(TE.drain(), "PE")
    c.wait(V, "PE", t1)
    V.tensor_copy(flagc[:], BK[0][:, 0:1])
    t_flagc = c.inc(V.drain(), "VE")

    xa = [xa0, xa1]
    t_xd = [None] * 9
    t_cons = [0] * 9
    t_trd = [0] * 9
    t_evq = [0, 0]
    qpar = 0
    t_xd[0] = c.dma(SY, xa[0][:], xh[0:128, :], "X0")
    t_xd[1] = c.dma(SY, xa[1][:], xh[128:256, :], "X1")
    for i in range(9):
        b = xa[i % 2]
        c.wait2(V, t_xd[i])
        V.reduce_sum(lncol[:, 0:1], b[:], axis=AXX)
        c.fence(V, "VE")
        V.tensor_scalar_mul(lncol[:, 1:2], lncol[:, 0:1], 1.0 / HID)
        c.fence(V, "VE")
        V.tensor_scalar(b[:], b[:], lncol[:, 1:2], None, ALU.subtract)
        t_sub = c.inc(V.drain(), "VE")
        c.wait(S_, "VE", t_sub)
        S_.activation(sqdump[:], b[:], AF.Square)
        t_sq = c.inc(S_.drain(), "SE")
        c.wait(V, "SE", t_sq)
        V.reduce_sum(lncol[:, 2:3], sqdump[:], axis=AXX)
        t_v2 = c.inc(V.drain(), "VE")
        c.wait(S_, "VE", t_v2)
        S_.activation(lncol[:, 3:4], lncol[:, 2:3], AF.Ln, bias=k_sc[:, 0:1], scale=1.0 / HID)
        c.fence(S_, "SE")
        S_.activation(lncol[:, 2:3], lncol[:, 3:4], AF.Exp, scale=-0.5)
        t_rs = c.inc(S_.drain(), "SE")
        c.wait(V, "SE", t_rs)
        if i >= 2:
            c.wait(V, "PE", t_trd[i - 2])
        if i >= 1:
            c.wait(V, "PE", t_trd[i - 1])
        V.tensor_scalar(xnb[:], b[:], lncol[:, 2:3], None, ALU.mult)
        t_xn = c.inc(V.drain(), "VE")
        t_cons[i] = t_xn
        if i + 2 < 9:
            c.wait(SY, "VE", t_xn)
            t_xd[i + 2] = c.dma(SY, xa[i % 2][:], xh[(i + 2) * 128:(i + 3) * 128, :], f"X{i % 2}")
        ncols = 128 if i < 8 else 3
        c.wait(TE, "VE", t_xn)
        c.wait2(TE, t_km)
        for r in range(4):
            q = [QA, QB][qpar]
            c.wait(TE, "SE", t_evq[qpar])
            for kk in range(8):
                k = r * 8 + kk
                TE.transpose(q[:, kk * 128:kk * 128 + 128], xnb[:, k * 128:(k + 1) * 128], k_id[:])
            t_t = c.inc(TE.drain(), "PE")
            c.wait(S_, "PE", t_t)
            for kk in range(8):
                k = r * 8 + kk
                S_.copy(xnT[:, k * 1028 + i * 128: k * 1028 + i * 128 + ncols],
                        q[:, kk * 128:kk * 128 + ncols])
            t_evq[qpar] = c.inc(S_.drain(), "SE")
            qpar ^= 1
        t_trd[i] = c.cnt["PE"]
    t_s0_done = c.inc(S_.drain(), "SE")
    if debug:
        c.wait(V, "SE", t_s0_done)
    if phases == 1:
        c.wait(SY, "SE", t_s0_done)
        c.dma(SY, outT, xT, "OUT")
        c.final_dma_waits(SY)
        es.close()
        return nc

    wsb = [wsb0, wsb1]
    ipw_off = []
    off = 0
    for cb in IP_BLK:
        nr = (NK * cb + RUN - 1) // RUN
        ipw_off.append((off, nr))
        off += nr * RUNP
    t_wip = [None] * 49
    t_blk_mm = [0] * 49
    t_chunk_ev = {}
    c.wait(GP, "PE", c.cnt["PE"])
    c.wait(GP, "VE", c.cnt["VE"])
    c.wait(GP, "SE", c.cnt["SE"])
    for b in range(min(2, 49)):
        o, nr = ipw_off[b]
        src = w_ip[:, o:o + nr * RUNP].rearrange("p (r q) -> p r q", q=RUNP)[:, :, 0:RUN]
        t_wip[b] = c.dma(GP, wsb[b % 2][:, 0:nr * RUN], src, f"W{b % 2}")
    t_pp_ev = [None, None]
    t_pp_mm = [0, 0]
    t_hs_dma = [None, None]
    t_sg_dma = [None, None]
    t_sgb = [0, 0]
    t_sgts = [0, 0]
    cpar = 0
    gpar = 0
    chunk_id = 0
    t_dtT = 0
    t_scan = 0
    for b in range(49):
        o, nr = ipw_off[b]
        cb = IP_BLK[b]
        nchk = cb // 128
        c.wait2(TE, t_wip[b])
        for j in range(nchk):
            cg = chunk_id
            pp = [P0, P1][cpar]
            if t_pp_ev[cpar] is not None:
                sem_n, tok_n = t_pp_ev[cpar]
                c.wait(TE, sem_n, tok_n)
            for k in range(NK):
                st = wsb[b % 2][:, k * cb + j * 128: k * cb + j * 128 + 128]
                TE.matmul(pp[:, 0:512], st, xnT[:, k * 1028:k * 1028 + 512],
                          start=(k == 0), stop=(k == NK - 1))
                TE.matmul(pp[:, 512:1024], st, xnT[:, k * 1028 + 512:k * 1028 + 1024],
                          start=(k == 0), stop=(k == NK - 1))
                TE.matmul(pp[:, 1024:1027], st, xnT[:, k * 1028 + 1024:k * 1028 + 1027],
                          start=(k == 0), stop=(k == NK - 1))
            t_pp_mm[cpar] = c.inc(TE.drain(), "PE")
            if cg < 64:
                c.wait(S_, "PE", t_pp_mm[cpar])
                c.wait(S_, "PE", t_sgb[gpar])
                S_.activation(sgbuf[:, gpar * 1024:(gpar + 1) * 1024], pp[:, 3:1027], AF.Silu)
                t_si = c.inc(S_.drain(), "SE")
                t_pp_ev[cpar] = ("SE", t_si)
                c.wait(TE, "SE", t_si)
                c.wait(TE, "SE", t_sgts[gpar])
                c.wait2(TE, t_sg_dma[gpar])
                q = [QA, QB][gpar]
                for z in range(8):
                    TE.transpose(q[:, z * 128:(z + 1) * 128],
                                 sgbuf[:, gpar * 1024 + z * 128: gpar * 1024 + (z + 1) * 128], k_id[:])
                t_sgb[gpar] = c.inc(TE.drain(), "PE")
                c.wait(S_, "PE", t_sgb[gpar])
                S_.copy(sgTs[:, gpar * 1024:(gpar + 1) * 1024], q[:, 0:1024])
                t_sgts[gpar] = c.inc(S_.drain(), "SE")
                c.wait(SY, "SE", t_sgts[gpar])
                src = sgTs[:, gpar * 1024:(gpar + 1) * 1024].rearrange("p (z ch) -> p z ch", z=8)
                dst = d_sg[:, :, cg * 128:(cg + 1) * 128].rearrange("z p ch -> p z ch")
                t_sg_dma[gpar] = c.dma(SY, dst, src, f"SG{gpar}")
                gpar ^= 1
            elif cg < 144:
                cc = cg - 64
                c.wait(V, "PE", t_pp_mm[cpar])
                V.tensor_scalar(cbuf[:], pp[:, 0:1024], k_convw[:, cc * 4:cc * 4 + 1], None, ALU.mult)
                V.scalar_tensor_tensor(cbuf[:], pp[:, 1:1025], k_convw[:, cc * 4 + 1:cc * 4 + 2], cbuf[:], ALU.mult, ALU.add)
                V.scalar_tensor_tensor(cbuf[:], pp[:, 2:1026], k_convw[:, cc * 4 + 2:cc * 4 + 3], cbuf[:], ALU.mult, ALU.add)
                V.scalar_tensor_tensor(cbuf[:], pp[:, 3:1027], k_convw[:, cc * 4 + 3:cc * 4 + 4], cbuf[:], ALU.mult, ALU.add)
                t_cv = c.inc(V.drain(), "VE")
                t_pp_ev[cpar] = ("VE", t_cv)
                hpar = cc % 2
                c.wait(S_, "VE", t_cv)
                c.wait2(S_, t_hs_dma[hpar])
                S_.activation(hbuf[:, hpar * 1024:(hpar + 1) * 1024], cbuf[:], AF.Silu,
                              bias=k_convb[:, cc:cc + 1])
                t_hb = c.inc(S_.drain(), "SE")
                c.wait(SY, "SE", t_hb)
                t_hs_dma[hpar] = c.dma(SY, d_hsbc[cc, :, 0:1024],
                                       hbuf[:, hpar * 1024:(hpar + 1) * 1024], f"HB{hpar}")
            else:
                c.wait(V, "PE", t_pp_mm[cpar])
                V.tensor_scalar(dbuf[:], pp[:, 3:1027], k_dtb[:], None, ALU.add)
                t_d0 = c.inc(V.drain(), "VE")
                t_pp_ev[cpar] = ("VE", t_d0)
                c.wait(S_, "VE", t_d0)
                S_.activation(tbuf[:], dbuf[:], AF.Abs)
                c.fence(S_, "SE")
                S_.activation(tbuf[:], tbuf[:], AF.Exp, scale=-1.0)
                c.fence(S_, "SE")
                S_.activation(tbuf[:], tbuf[:], AF.Ln, bias=k_sc[:, 1:2])
                t_d1 = c.inc(S_.drain(), "SE")
                c.wait(V, "SE", t_d1)
                V.tensor_scalar_max(dtcm[:], dbuf[:], 0.0)
                c.fence(V, "VE")
                V.tensor_add(dtcm[:], dtcm[:], tbuf[:])
                c.fence(V, "VE")
                V.tensor_scalar_min(dtcm[:], dtcm[:], 100.0)
                c.fence(V, "VE")
                V.tensor_copy(sgbuf[:, 0:1024], dtcm[:])
                t_dtc = c.inc(V.drain(), "VE")
                c.wait(TE, "VE", t_dtc)
                c.wait(TE, "SE", c.cnt["SE"])
                for z in range(8):
                    TE.transpose(QA[:, z * 128:(z + 1) * 128],
                                 sgbuf[:, z * 128:(z + 1) * 128], k_id[:])
                t_tz = c.inc(TE.drain(), "PE")
                c.wait(S_, "PE", t_tz)
                S_.copy(dtTb[:], QA[:, 0:1024])
                t_dtT = c.inc(S_.drain(), "SE")
                c.wait(V, "VE", t_acol)
                V.tensor_scalar(dtcm[:], dtcm[:], acolh[:], None, ALU.mult)
                c.fence(V, "VE")
                for z in range(8):
                    V.tensor_tensor_scan(acum[:, z * 128:(z + 1) * 128],
                                         dtcm[:, z * 128:(z + 1) * 128],
                                         dtcm[:, z * 128:(z + 1) * 128],
                                         0.0, ALU.add, ALU.bypass)
                t_scan = c.inc(V.drain(), "VE")
            chunk_id += 1
            cpar ^= 1
        t_blk_mm[b] = c.cnt["PE"]
        if b + 2 < 49:
            o2, nr2 = ipw_off[b + 2]
            c.wait(GP, "PE", t_blk_mm[b])
            src = w_ip[:, o2:o2 + nr2 * RUNP].rearrange("p (r q) -> p r q", q=RUNP)[:, :, 0:RUN]
            t_wip[b + 2] = c.dma(GP, wsb[b % 2][:, 0:nr2 * RUN], src, f"W{b % 2}")

    if debug:
        c.wait(SY, "VE", t_scan)
        c.wait(SY, "SE", t_dtT)
        c.wait2(SY, t_sg_dma[0])
        c.wait2(SY, t_sg_dma[1])
        c.wait2(SY, t_hs_dma[0])
        c.wait2(SY, t_hs_dma[1])
        c.dma(SY, dbg["dt"], dtcm[:], "DBG")
        c.dma(SY, dbg["acum"], acum[:], "DBG")
        c.dma(SY, dbg["hsbc"][:, :, 0:1024], d_hsbc[:, :, 0:1024], "DBG")
        c.dma(SY, dbg["sg"][:, :, 0:8192], d_sg[:, :, 0:8192], "DBG")
    if phases == 2:
        c.dma(SY, outT, xT, "OUT")
        c.final_dma_waits(SY)
        es.close()
        return nc

    def emit_ablock(z, want_dec, want_expA, want_splTI):
        acz = acum[:, z * 128:(z + 1) * 128]
        c.wait(V, "VE", t_scan)
        c.wait(TE, "VE", c.cnt["VE"])
        V.tensor_copy(spl3[:, 0:128], acz)
        V.tensor_tensor(splt[:, 0:128], acz, spl3[:, 0:128], ALU.subtract)
        V.tensor_copy(spl3[:, 128:256], splt[:, 0:128])
        V.tensor_tensor(splt[:, 128:256], splt[:, 0:128], spl3[:, 128:256], ALU.subtract)
        V.tensor_copy(spl3[:, 256:384], splt[:, 128:256])
        t_sp = c.inc(V.drain(), "VE")
        c.wait(TE, "VE", t_sp)
        for kk in range(3):
            TE.transpose(QA[:, kk * 128:(kk + 1) * 128], spl3[:, kk * 128:(kk + 1) * 128], k_id[:])
        t_tr = c.inc(TE.drain(), "PE")
        c.wait(V, "PE", t_tr)
        V.tensor_copy(acT[:], QA[:, 0:128])
        V.tensor_tensor(acT[:], acT[:], QA[:, 128:256], ALU.add)
        V.tensor_tensor(acT[:], acT[:], QA[:, 256:384], ALU.add)
        if want_splTI:
            V.tensor_copy(splTI[:, 0:128], acT[:])
            V.tensor_tensor(splt[:, 0:128], acT[:], splTI[:, 0:128], ALU.subtract)
            V.tensor_copy(splTI[:, 128:256], splt[:, 0:128])
            V.tensor_tensor(splt[:, 128:256], splt[:, 0:128], splTI[:, 128:256], ALU.subtract)
            V.tensor_copy(splTI[:, 256:384], splt[:, 128:256])
        t_acT = c.inc(V.drain(), "VE")
        V.tensor_copy(az3[:, 0:1], acz[:, 127:128])
        c.fence(V, "VE")
        V.tensor_tensor(azcol[:, 0:1], acz[:, 127:128], az3[:, 0:1], ALU.subtract)
        c.fence(V, "VE")
        V.tensor_copy(az3[:, 1:2], azcol[:, 0:1])
        c.fence(V, "VE")
        V.tensor_tensor(azcol[:, 1:2], azcol[:, 0:1], az3[:, 1:2], ALU.subtract)
        c.fence(V, "VE")
        V.tensor_copy(az3[:, 2:3], azcol[:, 1:2])
        t_az = c.inc(V.drain(), "VE")
        c.wait(TE, "VE", t_az)
        TE.transpose(QB[0:3, 0:128], az3[:, 0:3], k_id[:])
        t_azt = c.inc(TE.drain(), "PE")
        c.wait(V, "PE", t_azt)
        V.tensor_copy(r3[:], QB[0:3, 0:128])
        t_r3 = c.inc(V.drain(), "VE")
        c.wait(TE, "VE", t_r3)
        TE.matmul(BK[4][:, 0:128], k_on3[:], r3[:], start=True, stop=True)
        t_azbc = c.inc(TE.drain(), "PE")
        c.wait(V, "PE", t_azbc)
        V.tensor_scalar_max(wrk[0:1, 0:128], BK[4][0:1, 0:128], -80.0)
        t_eazc = c.inc(V.drain(), "VE")
        c.wait(S_, "VE", t_eazc)
        S_.activation(eaz1[:], wrk[0:1, 0:128], AF.Exp)
        t_e1 = c.inc(S_.drain(), "SE")
        c.wait(V, "SE", t_e1)
        V.tensor_copy(eaz2[0:1, 0:128], eaz1[:])
        c.fence(V, "VE")
        V.tensor_tensor(wrk[0:1, 0:128], eaz1[:], eaz2[0:1, 0:128], ALU.subtract)
        c.fence(V, "VE")
        V.tensor_copy(eaz2[0:1, 128:256], wrk[0:1, 0:128])
        t_e2 = c.inc(V.drain(), "VE")
        c.wait(TE, "VE", t_e2)
        TE.matmul(BK[5][:, 0:128], k_on1[0:1, 0:128], eaz2[0:1, 0:128], start=True, stop=False)
        TE.matmul(BK[5][:, 0:128], k_on1[0:1, 0:128], eaz2[0:1, 128:256], start=False, stop=True)
        t_ebc = c.inc(TE.drain(), "PE")
        c.wait(V, "PE", t_ebc)
        c.wait(V, "VE", t_acT)
        V.tensor_copy(eazb[:], BK[5][:, 0:128])
        if want_dec:
            V.tensor_tensor(wrk2[:], BK[4][:, 0:128], acT[:], ALU.subtract)
            V.tensor_scalar_max(wrk2[:], wrk2[:], -80.0)
            t_w = c.inc(V.drain(), "VE")
            c.wait(S_, "VE", t_w)
            S_.activation(dec_zT[:], wrk2[:], AF.Exp)
        if want_expA:
            V.tensor_scalar_max(ltw[:], acT[:], -80.0)
            t_w2 = c.inc(V.drain(), "VE")
            c.wait(S_, "VE", t_w2)
            S_.activation(expA[:], ltw[:], AF.Exp)
        t_se = c.inc(S_.drain(), "SE")
        t_ve = c.inc(V.drain(), "VE")
        return t_se, t_ve

    V.memset(prev[:], 0.0)
    t_pm = c.inc(V.drain(), "VE")
    c.wait(V, "VE", t_pm)
    t_xs = [None] * (NZ + 1)
    t_ss = [None] * (NZ + 1)
    for z in range(NZ):
        c.wait2(SY, t_xs[z])
        c.wait2(SY, t_ss[z])
        c.wait(SY, "PE", c.cnt["PE"])
        srcB = d_hsbc[64:72, :, z * 128:(z + 1) * 128]
        t_bz = c.dma(SY, bzb[:].rearrange("p (k t) -> p k t", k=8),
                     srcB.rearrange("k p t -> p k t"), "BZ")
        srcH0 = d_hsbc[0:32, :, z * 128:(z + 1) * 128]
        t_h0 = c.dma(SY, hszh[:].rearrange("p (k t) -> p k t", k=32),
                     srcH0.rearrange("k p t -> p k t"), "HS")
        t_ab = emit_ablock(z, want_dec=True, want_expA=False, want_splTI=False)
        c.wait(V, "SE", t_ab[0])
        c.wait(V, "SE", t_dtT)
        V.tensor_tensor(ddz[:], dtTb[:, z * 128:(z + 1) * 128], dec_zT[:], ALU.mult)
        t_ddz = c.inc(V.drain(), "VE")
        t_qfree = [t_ddz, t_ddz]
        t_sfree = [t_ddz, t_ddz]
        t_btm = [0, 0]
        for sb in range(16):
            g = sb // 2
            par = sb % 2
            h0 = 8 * sb
            QX = [QA, QB][par]
            SBK = [BK[0], BK[1]][par]
            if sb == 8:
                c.wait(SY, "PE", c.cnt["PE"])
                srcH1 = d_hsbc[32:64, :, z * 128:(z + 1) * 128]
                t_h0 = c.dma(SY, hszh[:].rearrange("p (k t) -> p k t", k=32),
                             srcH1.rearrange("k p t -> p k t"), "HS")
            if sb % 2 == 0:
                gp = (sb // 2) % 2
                c.wait2(TE, t_bz)
                c.wait(TE, "VE", t_btm[gp])
                c.wait(TE, "VE", t_qfree[1])
                TE.transpose(QB[:, 512:640], bzb[:, g * 128:(g + 1) * 128], k_id[:])
                t_bq = c.inc(TE.drain(), "PE")
                c.wait(V, "PE", t_bq)
                V.tensor_copy(btm[:, gp * 128:(gp + 1) * 128], QB[:, 512:640])
                t_btm[gp] = c.inc(V.drain(), "VE")
            c.wait2(TE, t_h0)
            c.wait(TE, "VE", t_qfree[par])
            c.wait(TE, "VE", t_btm[(sb // 2) % 2])
            for q in range(4):
                TE.transpose(QX[:, q * 128:(q + 1) * 128],
                             hszh[:, ((4 * sb + q) % 32) * 128:((4 * sb + q) % 32 + 1) * 128],
                             k_id[:])
            t_pt = c.inc(TE.drain(), "PE")
            c.wait(V, "PE", t_pt)
            c.wait(V, "VE", t_ddz)
            if sb == 0:
                c.wait2(V, t_xs[z])
                c.wait2(V, t_ss[z])
            for kk in range(8):
                h = h0 + kk
                V.tensor_scalar(xzb[:, h * 64:(h + 1) * 64], QX[:, kk * 64:(kk + 1) * 64],
                                dtTb[:, z * 128 + h:z * 128 + h + 1], None, ALU.mult)
            for kk in range(8):
                h = h0 + kk
                V.tensor_scalar(xch2[:, (par * 8 + kk) * 64:(par * 8 + kk + 1) * 64],
                                QX[:, kk * 64:(kk + 1) * 64],
                                ddz[:, h:h + 1], None, ALU.mult)
            t_x2 = c.inc(V.drain(), "VE")
            t_qfree[par] = t_x2
            c.wait(TE, "VE", t_x2)
            c.wait(TE, "VE", t_sfree[par])
            TE.matmul(SBK[:, 0:512], btm[:, ((sb // 2) % 2) * 128:((sb // 2) % 2 + 1) * 128],
                      xch2[:, par * 512:(par + 1) * 512], start=True, stop=True)
            t_sm = c.inc(TE.drain(), "PE")
            c.wait(V, "PE", t_sm)
            V.tensor_copy(szb[:, sb * 512:(sb + 1) * 512], SBK[:, 0:512])
            for kk in range(8):
                h = h0 + kk
                V.scalar_tensor_tensor(prev[:, h * 64:(h + 1) * 64],
                                       prev[:, h * 64:(h + 1) * 64],
                                       eazb[:, h:h + 1], SBK[:, kk * 64:(kk + 1) * 64],
                                       ALU.mult, ALU.add)
            t_sv = c.inc(V.drain(), "VE")
            t_sfree[par] = t_sv
        t_zv = c.inc(V.drain(), "VE")
        c.wait(SY, "VE", t_zv)
        t_xs[z + 1] = c.dma(SY, d_xch[z], xzb[:], "XS")
        t_ss[z + 1] = c.dma(SY, d_s[z], szb[:], "SS")
    c.wait2(V, t_xs[NZ])
    c.wait2(V, t_ss[NZ])
    V.tensor_copy(xzb[:], prev[:])
    t_xo = c.inc(V.drain(), "VE")
    c.wait(SY, "VE", t_xo)
    t_ci = c.dma(SY, d_ccin, xzb[:], "CI")
    c.wait2(GP, t_ci)
    GP.collective_compute("AllGather", ALU.bypass,
                          replica_groups=[[0, 4], [1, 5], [2, 6], [3, 7]],
                          ins=[d_ccin], outs=[d_ccout]).then_inc(c.sem["CC"], 1)
    c.cnt["CC"] += 1
    if debug:
        c.wait(SY, "VE", c.cnt["VE"])
        c.dma(SY, dbg["prev"], prev[:], "DBG")
        c.dma(SY, dbg["xch"], d_xch, "DBG")
        c.dma(SY, dbg["s"], d_s, "DBG")
    if phases == 3:
        c.dma(SY, outT, xT, "OUT")
        c.final_dma_waits(SY)
        es.close()
        return nc

    c.wait(GP, "PE", c.cnt["PE"])
    t_rt = c.dma(GP, btm[:, 0:256],
                 w_rt.rearrange("p (r q) -> p r q", q=RUNP)[:, 0:1, 0:256], "RT")
    t_ald = c.dma(GP, hszh[:],
                  w_al.rearrange("p (r q) -> p r q", q=RUNP)[:, :, 0:RUN], "AL")
    c.wait2(GP, t_ci)
    t_bfl = c.dma(GP, bfl[:],
                  w_bl.rearrange("p (r q) -> p r q", q=RUNP)[:, :, 0:RUN], "BL")
    t_bb0 = c.dma(GP, bbasebf[:],
                  c_bb.rearrange("p (r q) -> p r q", q=RUNP)[:, :, 0:RUN], "BB")
    t_rtev = 0
    for i in range(8):
        c.wait2(TE, t_rt)
        c.wait(TE, "VE", t_rtev)
        for k in range(NK):
            TE.matmul(BK[2][:, 0:8], xnT[:, k * 1028 + 3 + i * 128:k * 1028 + 3 + (i + 1) * 128],
                      btm[:, k * 8:(k + 1) * 8], start=(k == 0), stop=(k == NK - 1))
        t_rm = c.inc(TE.drain(), "PE")
        c.wait(V, "PE", t_rm)
        V.tensor_copy(tpw[:, i * 8:(i + 1) * 8], BK[2][:, 0:8])
        t_rtev = c.inc(V.drain(), "VE")
        c.fence(V, "VE")
        V.reduce_max(tpc[:, 0:1], tpw[:, i * 8:(i + 1) * 8], axis=AXX)
        c.fence(V, "VE")
        V.tensor_scalar_mul(tpc[:, 2:3], tpc[:, 0:1], -1.0)
        c.fence(V, "VE")
        V.tensor_scalar(tpc[:, 8:16], tpw[:, i * 8:(i + 1) * 8], tpc[:, 0:1], None, ALU.is_equal)
        c.fence(V, "VE")
        V.scalar_tensor_tensor(sumg[:], tpc[:, 8:16], -1e30, tpw[:, i * 8:(i + 1) * 8],
                               ALU.mult, ALU.add)
        c.fence(V, "VE")
        V.reduce_max(tpc[:, 1:2], sumg[:], axis=AXX)
        t_m2 = c.inc(V.drain(), "VE")
        c.wait(S_, "VE", t_m2)
        S_.activation(sumg[:], tpw[:, i * 8:(i + 1) * 8], AF.Exp, bias=tpc[:, 2:3])
        c.fence(S_, "SE")
        S_.activation(tpc[:, 3:4], tpc[:, 1:2], AF.Exp, bias=tpc[:, 2:3])
        t_se2 = c.inc(S_.drain(), "SE")
        c.wait(V, "SE", t_se2)
        V.tensor_scalar(tpc[:, 8:16], tpw[:, i * 8:(i + 1) * 8], tpc[:, 1:2], None, ALU.is_ge)
        c.fence(V, "VE")
        V.tensor_tensor(sumg[:], sumg[:], tpc[:, 8:16], ALU.mult)
        c.fence(V, "VE")
        V.tensor_scalar(tpc[:, 4:5], tpc[:, 3:4], 1.0, None, ALU.add)
        c.fence(V, "VE")
        V.reciprocal(tpc[:, 5:6], tpc[:, 4:5])
        c.fence(V, "VE")
        V.tensor_scalar(comb[:, i * 8:(i + 1) * 8], sumg[:], tpc[:, 5:6], 2.0,
                        ALU.mult, ALU.mult)
        c.fence(V, "VE")
    V.tensor_copy(combb[:], comb[:])
    t_cb = c.inc(V.drain(), "VE")
    c.wait(TE, "VE", t_cb)
    for i in range(8):
        TE.transpose(QA[0:8, i * 128:(i + 1) * 128], combb[:, i * 8:(i + 1) * 8], k_id[:])
    t_ct = c.inc(TE.drain(), "PE")
    c.wait(V, "PE", t_ct)
    V.tensor_copy(hwT[0:8, 0:1024], QA[0:8, 0:1024])
    t_cm = c.inc(V.drain(), "VE")
    c.wait(SY, "VE", t_cm)
    t_cd = c.dma(SY, d_comb, hwT[0:8, 0:1024], "CD")
    t_crmm = 0
    for e in range(8):
        c.wait2(SY, t_cd)
        c.wait(SY, "PE", t_crmm)
        t_cr = c.dma(SY, combrow[:], d_comb[e:e + 1, :], "CR")
        c.wait2(TE, t_cr)
        TE.matmul(BK[2][:, 0:512], k_em[0:1, e * 128:(e + 1) * 128],
                  combrow[0:1, 0:512], start=(e == 0), stop=(e == 7))
        TE.matmul(BK[3][:, 0:512], k_em[0:1, e * 128:(e + 1) * 128],
                  combrow[0:1, 512:1024], start=(e == 0), stop=(e == 7))
        t_crmm = c.inc(TE.drain(), "PE")
    t_crm = c.cnt["PE"]
    c.wait(V, "PE", t_crm)
    V.tensor_copy(crep[:, 0:512], BK[2][:, 0:512])
    V.tensor_copy(crep[:, 512:1024], BK[3][:, 0:512])
    t_crep = c.inc(V.drain(), "VE")
    t_hw = 0
    for tg in range(2):
        c.wait2(TE, t_ald)
        c.wait(TE, "VE", t_hw)
        for k in range(NK):
            TE.matmul(BK[4][:, 0:512], hszh[:, k * 128:(k + 1) * 128],
                      xnT[:, k * 1028 + 3 + tg * 512:k * 1028 + 3 + (tg + 1) * 512],
                      start=(k == 0), stop=(k == NK - 1))
        t_hm = c.inc(TE.drain(), "PE")
        c.wait(V, "PE", t_hm)
        c.wait(V, "VE", t_crep)
        c.wait2(V, t_cd)
        V.tensor_tensor(hwT[:, tg * 512:(tg + 1) * 512], BK[4][:, 0:512],
                        crep[:, tg * 512:(tg + 1) * 512], ALU.mult)
        t_hw = c.inc(V.drain(), "VE")
    t_hwd = c.inc(V.drain(), "VE")
    wfb = [wfb0, wfb1]
    t_wf = [None] * 16
    t_fbm = [0] * 16
    for ob in range(min(2, 16)):
        src = w_fb[:, ob * 4 * RUNP:(ob + 1) * 4 * RUNP].rearrange(
            "p (r q) -> p r q", q=RUNP)[:, :, 0:RUN]
        t_wf[ob] = c.dma(GP, wfb[ob % 2][:], src, f"WF{ob % 2}")
    t_gfev = [0, 0, 0, 0]
    t_gfdma = [None, None, None, None]
    for ob in range(16):
        c.wait2(TE, t_wf[ob])
        for o2 in range(2):
            o128 = ob * 2 + o2
            for tg in range(2):
                slot = (o2 * 2 + tg)
                pb_ = BK[slot]
                c.wait(TE, "SE", t_gfev[slot])
                for k in range(NK):
                    TE.matmul(pb_[:, 0:512],
                              wfb[ob % 2][:, k * 256 + o2 * 128:k * 256 + (o2 + 1) * 128],
                              xnT[:, k * 1028 + 3 + tg * 512:k * 1028 + 3 + (tg + 1) * 512],
                              start=(k == 0), stop=False)
                c.wait(TE, "VE", t_hwd)
                c.wait2(TE, t_bfl)
                TE.matmul(pb_[:, 0:512], bfl[:, o128 * 128:(o128 + 1) * 128],
                          hwT[:, tg * 512:(tg + 1) * 512], start=False, stop=False)
                c.wait2(TE, t_bb0)
                TE.matmul(pb_[:, 0:512], bbasebf[0:1, o128 * 128:(o128 + 1) * 128],
                          k_on1[0:1, 0:512], start=False, stop=True)
                t_m = c.inc(TE.drain(), "PE")
                c.wait(S_, "PE", t_m)
                c.wait2(S_, t_gfdma[slot])
                S_.activation(sgbuf[:, slot * 512:(slot + 1) * 512], pb_[:, 0:512], AF.Silu)
                t_gfev[slot] = c.inc(S_.drain(), "SE")
                c.wait(SY, "SE", t_gfev[slot])
                t_gfdma[slot] = c.dma(SY, d_gffn[o128 * 128:(o128 + 1) * 128,
                                                 tg * 512:(tg + 1) * 512],
                                      sgbuf[:, slot * 512:(slot + 1) * 512], f"GF{slot}")
        if ob + 2 < 16:
            c.wait(GP, "PE", c.cnt["PE"])
            src = w_fb[:, (ob + 2) * 4 * RUNP:(ob + 3) * 4 * RUNP].rearrange(
                "p (r q) -> p r q", q=RUNP)[:, :, 0:RUN]
            t_wf[ob + 2] = c.dma(GP, wfb[ob % 2][:], src, f"WF{ob % 2}")
    if debug:
        for sl in range(4):
            c.wait2(SY, t_gfdma[sl])
        c.dma(SY, dbg["gffn"][:, 0:1024], d_gffn[:, 0:1024], "DBG")
    if phases == 4:
        c.dma(SY, outT, xT, "OUT")
        c.final_dma_waits(SY)
        es.close()
        return nc

    c.wait(SY, "CC", 1)
    c.wait(SY, "PE", c.cnt["PE"])
    c.wait(SY, "SE", c.cnt["SE"])
    t_xi = c.dma(SY, xinm[:], d_ccout[0:128, :], "XIN")
    c.wait2(V, t_xi)
    c.wait(V, "VE", t_flagc)
    c.wait(V, "PE", c.cnt["PE"])
    V.tensor_scalar(xinm[:], xinm[:], flagc[:], None, ALU.mult)
    t_xm = c.inc(V.drain(), "VE")
    c.wait(V, "VE", t_xm)
    V.tensor_copy(prevC[:], xinm[:])
    t_pv0 = c.inc(V.drain(), "VE")
    c.wait(V, "VE", t_pv0)
    t_zc = [None] * (NZ + 1)
    t_prevz = t_pv0
    t_zend = 0
    for z in range(NZ):
        c.wait(SY, "VE", t_zend)
        t_sgz = c.dma(SY, sg_z[:], d_sg[z, :, 0:8192], "SGZ")
        t_xcz = c.dma(SY, xch_z[:], d_xch[z], "XCZ")
        t_ssz = c.dma(SY, s_z[:], d_s[z], "SSZ")
        for q4 in range(4):
            t_hsz = c.dma(SY, hs_z[:, q4 * 2048:(q4 + 1) * 2048].rearrange(
                "p (k t) -> p k t", k=16),
                d_hsbc[q4 * 16:(q4 + 1) * 16, :, z * 128:(z + 1) * 128].rearrange(
                    "k p t -> p k t"), "HSZ")
        t_bzc = c.dma(SY, bz_c[:].rearrange("p (k t) -> p k t", k=8),
                      d_hsbc[64:72, :, z * 128:(z + 1) * 128].rearrange(
                          "k p t -> p k t"), "BZC")
        t_czc = c.dma(SY, cz_c[:].rearrange("p (k t) -> p k t", k=8),
                      d_hsbc[72:80, :, z * 128:(z + 1) * 128].rearrange(
                          "k p t -> p k t"), "CZC")
        t_ab = emit_ablock(z, want_dec=False, want_expA=True, want_splTI=True)
        c.wait2(V, t_hsz)
        for j in range(64):
            V.tensor_scalar(hs_z[:, j * 128:(j + 1) * 128], hs_z[:, j * 128:(j + 1) * 128],
                            k_dcol[:, j:j + 1], None, ALU.mult)
        t_hsd = c.inc(V.drain(), "VE")
        t_gtev = [t_ab[1], t_ab[1]]
        for g in range(8):
            gb = [BK[1], BK[5]][g % 2]
            c.wait2(TE, t_bzc)
            c.wait2(TE, t_czc)
            c.wait(TE, "VE", t_gtev[g % 2])
            TE.matmul(gb[:, 0:128], bz_c[:, g * 128:(g + 1) * 128],
                      cz_c[:, g * 128:(g + 1) * 128], start=True, stop=True)
            t_g = c.inc(TE.drain(), "PE")
            c.wait(V, "PE", t_g)
            V.tensor_tensor(gtm[:, g * 128:(g + 1) * 128], gb[:, 0:128], k_tri[:], ALU.mult)
            t_gtev[g % 2] = c.inc(V.drain(), "VE")
        t_gtm = c.cnt["VE"]
        t_qfree = [t_gtm, t_gtm]
        t_y1free = [t_gtm, t_gtm]
        t_y2free = [t_gtm, t_gtm]
        t_lwprev = t_gtm
        t_y2pe = 0
        t_w8free = 0
        for sb in range(16):
            g = sb // 2
            par = sb % 2
            QX = [QA, QB][par]
            Y1B = [BK[0], BK[2]][par]
            Y2B = [BK[3], BK[1]][par]
            h0 = 8 * sb
            c.wait(TE, "VE", t_ab[1])
            c.wait(TE, "VE", t_qfree[par])
            for kk in range(8):
                TE.transpose(QX[0:3, kk * 128:(kk + 1) * 128],
                             splTI[:, h0 + kk:h0 + kk + 257:128], k_id[:])
            t_lt = c.inc(TE.drain(), "PE")
            c.wait(V, "PE", t_lt)
            V.tensor_copy(r3b8[:], QX[0:3, 0:1024])
            t_r3 = c.inc(V.drain(), "VE")
            c.wait(TE, "VE", t_r3)
            c.wait(TE, "VE", t_lwprev)
            for kk in range(8):
                RB = [BK[4], BK[5]][kk // 4]
                TE.matmul(RB[:, (kk % 4) * 128:(kk % 4 + 1) * 128], k_on3[:],
                          r3b8[:, kk * 128:(kk + 1) * 128], start=True, stop=True)
            t_rb = c.inc(TE.drain(), "PE")
            c.wait(V, "PE", t_rb)
            for kk in range(8):
                RB = [BK[4], BK[5]][kk // 4]
                V.tensor_scalar(ltw8[:, kk * 128:(kk + 1) * 128],
                                RB[:, (kk % 4) * 128:(kk % 4 + 1) * 128],
                                acT[:, h0 + kk:h0 + kk + 1], 0.0,
                                ALU.subtract, ALU.min)
            V.tensor_scalar_max(ltw8[:], ltw8[:], -80.0)
            t_lw = c.inc(V.drain(), "VE")
            t_lwprev = t_lw
            c.wait(S_, "VE", t_lw)
            S_.activation(ltw8[:], ltw8[:], AF.Exp)
            t_ex = c.inc(S_.drain(), "SE")
            c.wait(S_, "VE", t_prevz)
            c.wait(S_, "PE", t_y2pe)
            S_.copy(pvb[:], prevC[:, sb * 512:(sb + 1) * 512])
            t_pv = c.inc(S_.drain(), "SE")
            c.wait(V, "SE", t_ex)
            for kk in range(8):
                V.tensor_tensor(mtb8[:, kk * 128:(kk + 1) * 128],
                                gtm[:, g * 128:(g + 1) * 128],
                                ltw8[:, kk * 128:(kk + 1) * 128], ALU.mult)
            t_mtb = c.inc(V.drain(), "VE")
            c.wait(TE, "VE", t_mtb)
            c.wait2(TE, t_xcz)
            c.wait(TE, "VE", t_y1free[par])
            for kk in range(8):
                TE.matmul(Y1B[:, kk * 64:(kk + 1) * 64], mtb8[:, kk * 128:(kk + 1) * 128],
                          xch_z[:, (h0 + kk) * 64:(h0 + kk + 1) * 64],
                          start=True, stop=True)
            c.wait(TE, "SE", t_pv)
            c.wait(TE, "VE", t_y2free[par])
            TE.matmul(Y2B[:, 0:512], cz_c[:, g * 128:(g + 1) * 128], pvb[:],
                      start=True, stop=True)
            c.wait(TE, "VE", t_hsd)
            for q in range(4):
                TE.transpose(QX[:, q * 128:(q + 1) * 128],
                             hs_z[:, (4 * sb + q) * 128:(4 * sb + q + 1) * 128], k_id[:])
            t_pe2 = c.inc(TE.drain(), "PE")
            t_y2pe = t_pe2
            c.wait(S_, "PE", t_pe2)
            c.wait(S_, "VE", t_w8free)
            for kk in range(8):
                S_.activation(wrk8[:, kk * 64:(kk + 1) * 64], Y2B[:, kk * 64:(kk + 1) * 64],
                              AF.Copy, scale=expA[:, h0 + kk:h0 + kk + 1])
            t_se2 = c.inc(S_.drain(), "SE")
            c.wait(V, "PE", t_pe2)
            c.wait(V, "SE", t_se2)
            if sb == 0:
                c.wait2(V, t_sgz)
                c.wait2(V, t_ssz)
            V.tensor_tensor(wrk8[:], wrk8[:], Y1B[:, 0:512], ALU.add)
            V.tensor_tensor(y_z[:, sb * 512:(sb + 1) * 512], wrk8[:], QX[:, 0:512], ALU.add)
            for kk in range(8):
                h = h0 + kk
                V.scalar_tensor_tensor(prevC[:, h * 64:(h + 1) * 64],
                                       prevC[:, h * 64:(h + 1) * 64],
                                       eazb[:, h:h + 1],
                                       s_z[:, h * 64:(h + 1) * 64], ALU.mult, ALU.add)
            tv = c.inc(V.drain(), "VE")
            t_w8free = tv
            t_qfree[par] = tv
            t_y1free[par] = tv
            t_y2free[par] = tv
        if debug and z == NZ - 1:
            tdv = c.inc(V.drain(), "VE")
            c.wait(SY, "VE", tdv)
            c.dma(SY, dbg["ylast"], y_z[:], "DBG")
            tdbg = c.dma(SY, dbg["gtm"], gtm[:], "DBG")
            c.wait2(V, tdbg)
        V.tensor_tensor(y_z[:], y_z[:], sg_z[:], ALU.mult)
        t_gt = c.inc(V.drain(), "VE")
        c.wait(S_, "VE", t_gt)
        t_red = [0, 0]
        for gi in range(8):
            sl = csc[:, (gi % 2) * 1024:(gi % 2 + 1) * 1024]
            c.wait(S_, "VE", t_red[gi % 2])
            S_.activation(sl, y_z[:, gi * 1024:(gi + 1) * 1024], AF.Square)
            t_sq = c.inc(S_.drain(), "SE")
            c.wait(V, "SE", t_sq)
            V.reduce_sum(sumg[:, gi:gi + 1], sl, axis=AXX)
            t_red[gi % 2] = c.inc(V.drain(), "VE")
        t_vr = c.inc(V.drain(), "VE")
        c.wait(S_, "VE", t_vr)
        S_.activation(rstd[:], sumg[:], AF.Ln, bias=k_sc[:, 0:1], scale=1.0 / 1024.0)
        c.fence(S_, "SE")
        S_.activation(rstd[:], rstd[:], AF.Exp, scale=-0.5)
        t_rs = c.inc(S_.drain(), "SE")
        c.wait(V, "SE", t_rs)
        for gi in range(8):
            V.tensor_scalar(y_z[:, gi * 1024:(gi + 1) * 1024],
                            y_z[:, gi * 1024:(gi + 1) * 1024],
                            rstd[:, gi:gi + 1], None, ALU.mult)
        t_gg = c.inc(V.drain(), "VE")
        c.wait(SY, "VE", t_gg)
        t_ggev = [0, 0]
        for p in range(64):
            par = p % 2
            QG = [QA, QB][par]
            c.wait(TE, "VE", t_gg if p < 2 else t_ggev[par])
            c.wait2(TE, t_zc[z])
            TE.transpose(QG[:, 0:128], y_z[:, p * 128:(p + 1) * 128], k_id[:])
            t_t = c.inc(TE.drain(), "PE")
            c.wait(V, "PE", t_t)
            V.tensor_scalar(ggTs[:, p * 128:(p + 1) * 128], QG[:, 0:128],
                            k_nw[:, p:p + 1], None, ALU.mult)
            t_ggev[par] = c.inc(V.drain(), "VE")
        t_gga = c.inc(V.drain(), "VE")
        c.wait(SY, "VE", t_gga)
        t_zc[z + 1] = c.dma(SY, d_ggt[:, :, z * 128:(z + 1) * 128].rearrange(
                                "k p t -> p k t"),
                            ggTs[:].rearrange("p (k t) -> p k t", k=64), "GGS")
        t_prevz = c.cnt["VE"]
        t_zend = c.cnt["VE"]
    t_dbg_pc = None
    if debug:
        c.wait(SY, "VE", c.cnt["VE"])
        c.dma(SY, dbg["yz"], prevC[:], "DBG")
        c.wait2(SY, t_zc[NZ])
        t_dbg_pc = c.dma(SY, dbg["ggt"][:, :, 0:1024], d_ggt[:, :, 0:1024], "DBG")
    if phases == 5:
        c.dma(SY, outT, xT, "OUT")
        c.final_dma_waits(SY)
        es.close()
        return nc

    c.wait2(SY, t_zc[NZ])
    c.wait(SY, "PE", c.cnt["PE"])
    c.wait(SY, "VE", c.cnt["VE"])
    c.wait(SY, "SE", c.cnt["SE"])
    t_ggl = None
    for q4 in range(4):
        t_ggl = c.dma(SY, ggT[:, q4 * 16384:(q4 + 1) * 16384].rearrange(
            "p (k t) -> p k t", k=16),
            d_ggt[q4 * 16:(q4 + 1) * 16, :, 0:1024].rearrange("k p t -> p k t"), "GGL")
    wdb = [wdb0, wdb1]
    c.wait2(GP, t_dbg_pc)
    c.wait(GP, "PE", c.cnt["PE"])
    c.wait(GP, "VE", c.cnt["VE"])
    c.wait(GP, "SE", c.cnt["SE"])
    t_wd = [None] * 32
    for ob in range(2):
        src = w_wo[:, ob * 4 * RUNP:(ob + 1) * 4 * RUNP].rearrange(
            "p (r q) -> p r q", q=RUNP)[:, :, 0:RUN]
        t_wd[ob] = c.dma(GP, wdb[ob % 2][:], src, f"WO{ob % 2}")
    t_oev = [0, 0, 0, 0]
    t_odma = [None, None]
    t_bufev = [0, 0]
    t_gxd = [None, None]
    for ob in range(32):
        c.wait2(TE, t_wd[ob])
        for tg in range(2):
            slot = (ob % 2) * 2 + tg
            c.wait(SY, "VE", t_bufev[tg])
            t_g1 = c.dma(SY, gft[:, tg * 512:(tg + 1) * 512],
                         d_gffn[ob * 128:(ob + 1) * 128, tg * 512:(tg + 1) * 512],
                         f"GT{tg}")
            t_g2 = c.dma(SY, xtt[:, tg * 512:(tg + 1) * 512],
                         xT[ob * 128:(ob + 1) * 128, tg * 512:(tg + 1) * 512],
                         f"XT{tg}")
            t_gxd[tg] = (t_g1, t_g2)
            pb_ = BK[slot]
            c.wait2(TE, t_ggl)
            c.wait(TE, "VE", t_oev[slot])
            for chk in range(64):
                TE.matmul(pb_[:, 0:512], wdb[ob % 2][:, chk * 128:(chk + 1) * 128],
                          ggT[:, chk * 1024 + tg * 512:chk * 1024 + (tg + 1) * 512],
                          start=(chk == 0), stop=(chk == 63))
            t_m = c.inc(TE.drain(), "PE")
            c.wait(V, "PE", t_m)
            c.wait2(V, t_gxd[tg][0])
            c.wait2(V, t_gxd[tg][1])
            c.wait2(V, t_odma[tg])
            V.tensor_tensor(otb[:, tg * 512:(tg + 1) * 512], pb_[:, 0:512],
                            gft[:, tg * 512:(tg + 1) * 512], ALU.mult)
            V.tensor_tensor(otb[:, tg * 512:(tg + 1) * 512],
                            otb[:, tg * 512:(tg + 1) * 512],
                            xtt[:, tg * 512:(tg + 1) * 512], ALU.add)
            t_oev[slot] = c.inc(V.drain(), "VE")
            t_bufev[tg] = t_oev[slot]
            c.wait(SY, "VE", t_oev[slot])
            t_odma[tg] = c.dma(SY, outT[ob * 128:(ob + 1) * 128, tg * 512:(tg + 1) * 512],
                               otb[:, tg * 512:(tg + 1) * 512], f"OS{tg}")
        if ob + 2 < 32:
            c.wait(GP, "PE", c.cnt["PE"])
            src = w_wo[:, (ob + 2) * 4 * RUNP:(ob + 3) * 4 * RUNP].rearrange(
                "p (r q) -> p r q", q=RUNP)[:, :, 0:RUN]
            t_wd[ob + 2] = c.dma(GP, wdb[ob % 2][:], src, f"WO{ob % 2}")
    c.final_dma_waits(SY)
    es.close()
    return nc


_CACHE = {}


def kernel(**inputs):
    st = _stage_weights(inputs)
    x = np.asarray(inputs["L0_attn_x"], np.float32)
    key = (_PHASES, _DEBUG)
    if key not in _CACHE:
        _CACHE[key] = build_nc(debug=_DEBUG, phases=_PHASES)
    nc = _CACHE[key]
    shared = {
        "w_ip": st["w_ip"], "w_fb": st["w_fb"], "w_wo": st["w_wo"],
        "w_al": st["w_al"], "w_bl": st["w_bl"], "w_rt": st["w_rt"],
        "c_convw": st["c_convw"], "c_convb": st["c_convb"],
        "c_dtb": st["c_dtb"], "c_alog": st["c_alog"], "c_dcol": st["c_dcol"],
        "c_nw": st["c_nw"], "c_bb": st["c_bb"], "c_id": st["c_id"],
        "c_tri": st["c_tri"], "c_on3": st["c_on3"], "c_on1": st["c_on1"],
        "c_em": st["c_em"], "c_sc": st["c_sc"],
    }
    in_maps = []
    for core in range(8):
        b, hf = core % 4, core // 4
        xh = np.zeros((XROWS, HID), np.float32)
        if hf == 1:
            xh[0:3] = x[b, 1021:1024]
        xh[3:1027] = x[b, hf * 1024:(hf + 1) * 1024]
        m = dict(shared)
        m["xh"] = xh
        m["xT"] = np.ascontiguousarray(x[b, hf * 1024:(hf + 1) * 1024].T)
        m["c_flag"] = np.array([[float(hf)]], np.float32)
        in_maps.append(m)
    res = run_bass_kernel_spmd(nc, in_maps, core_ids=list(range(8)))
    out = np.empty((4, 2048, HID), np.float32)
    for core in range(8):
        b, hf = core % 4, core // 4
        out[b, hf * 1024:(hf + 1) * 1024] = res.results[core]["outT"].T
    return out


# version 09 (final)
# speedup vs baseline: 1.0207x; 1.0207x over previous
import numpy as np
from contextlib import ExitStack
import concourse.bass as bass
import concourse.mybir as mybir
from concourse.bass_utils import run_bass_kernel_spmd

F32 = mybir.dt.float32
BF16 = mybir.dt.bfloat16
AF = mybir.ActivationFunctionType
ALU = mybir.AluOpType
AXX = mybir.AxisListType.X

HID = 4096
T = 1024
XROWS = 1152
NK = 32
INTER = 8192
NPROJ = 18560
NZ = 8
PD = 64
RUN = 2048
RUNP = 2052
IP_BLK = [384] * 48 + [128]
EPS = 1e-5

_PHASES = 6
_DEBUG = False


def _runs(a):
    a = np.ascontiguousarray(a, dtype=np.float32)
    M = a.shape[1]
    nr = (M + RUN - 1) // RUN
    out = np.zeros((128, nr, RUNP), np.float32)
    pad = np.zeros((128, nr * RUN), np.float32)
    pad[:, :M] = a
    out[:, :, :RUN] = pad.reshape(128, nr, RUN)
    return out


def _stage_weights(ins):
    st = {}
    w = np.asarray(ins["L0_attn_in_proj_w"], np.float32)
    arr = w.reshape(NPROJ, NK, 128).transpose(2, 1, 0)
    blobs, ip_off, off = [], [], 0
    for b, cb in enumerate(IP_BLK):
        chb = sum(IP_BLK[:b])
        sub = np.ascontiguousarray(arr[:, :, chb:chb + cb]).reshape(128, NK * cb)
        r = _runs(sub)
        ip_off.append((off, r.shape[1]))
        off += r.shape[1] * RUNP
        blobs.append(r.reshape(128, -1))
    st["w_ip"] = np.concatenate(blobs, axis=1)
    st["ip_off"] = ip_off

    wb = np.asarray(ins["L0_ffn_W_base"], np.float32)
    arr = wb.reshape(HID, NK, 128).transpose(2, 1, 0)
    blobs = []
    for ob in range(16):
        sub = np.ascontiguousarray(arr[:, :, ob * 256:(ob + 1) * 256]).reshape(128, NK * 256)
        blobs.append(_runs(sub).reshape(128, -1))
    st["w_fb"] = np.concatenate(blobs, axis=1)

    wo = np.asarray(ins["L0_attn_out_proj_w"], np.float32)
    arr = wo.reshape(HID, 64, 128).transpose(2, 1, 0)
    blobs = []
    for ob in range(32):
        sub = np.ascontiguousarray(arr[:, :, ob * 128:(ob + 1) * 128]).reshape(128, 64 * 128)
        blobs.append(_runs(sub).reshape(128, -1))
    st["w_wo"] = np.concatenate(blobs, axis=1)

    A = np.asarray(ins["L0_ffn_A"], np.float32).reshape(128, HID)
    arr = np.ascontiguousarray(A.T).reshape(NK, 128, 128).transpose(1, 0, 2)
    st["w_al"] = _runs(np.ascontiguousarray(arr).reshape(128, NK * 128)).reshape(128, -1)

    Bm = np.asarray(ins["L0_ffn_B"], np.float32)
    st["w_bl"] = _runs(np.transpose(Bm, (0, 2, 1)).reshape(128, HID)).reshape(128, -1)

    wr = np.asarray(ins["L0_ffn_W_router"], np.float32)
    arr = np.ascontiguousarray(wr.T).reshape(NK, 128, 8).transpose(1, 0, 2)
    st["w_rt"] = _runs(np.ascontiguousarray(arr).reshape(128, NK * 8)).reshape(128, -1)

    cw = np.asarray(ins["L0_attn_conv_w"], np.float32).reshape(10240, 4)
    st["c_convw"] = np.ascontiguousarray(cw.reshape(80, 128, 4).transpose(1, 0, 2).reshape(128, 320))
    st["c_convb"] = np.ascontiguousarray(np.asarray(ins["L0_attn_conv_b"], np.float32).reshape(80, 128).T)
    st["c_dtb"] = np.asarray(ins["L0_attn_dt_bias"], np.float32).reshape(128, 1)
    st["c_alog"] = np.asarray(ins["L0_attn_A_log"], np.float32).reshape(128, 1)
    D = np.asarray(ins["L0_attn_D"], np.float32)
    st["c_dcol"] = np.ascontiguousarray(D[(np.arange(INTER) // PD)].reshape(64, 128).T)
    st["c_drep"] = _runs(np.repeat(st["c_dcol"], 128, axis=1)).reshape(128, -1)
    st["c_nw"] = np.ascontiguousarray(np.asarray(ins["L0_attn_norm_w"], np.float32).reshape(64, 128).T)
    bb = np.zeros((1, 2, RUNP), np.float32)
    bb[0, :, :RUN] = np.asarray(ins["L0_ffn_b_base"], np.float32).reshape(2, RUN)
    st["c_bb"] = bb.reshape(1, 2 * RUNP)
    st["c_id"] = np.eye(128, dtype=np.float32)
    st["c_tri"] = np.triu(np.ones((128, 128), np.float32))
    st["c_on3"] = np.ones((3, 128), np.float32)
    st["c_on1"] = np.ones((1, 512), np.float32)
    sc = np.zeros((128, 2), np.float32)
    sc[:, 0] = 1e-5
    sc[:, 1] = 1.0
    st["c_sc"] = sc
    em = np.zeros((1, 1024), np.float32)
    for e in range(8):
        em[0, e * 128 + e * 16:e * 128 + (e + 1) * 16] = 1.0
    st["c_em"] = em
    return st


class Ctx:
    def __init__(self, nc):
        self.nc = nc
        self.sem = {}
        self.cnt = {}

    def mksem(self, name):
        self.sem[name] = self.nc.alloc_semaphore(name=name)
        self.cnt[name] = 0

    def dma(self, q, out, in_, key, **kw):
        k = "D" + key
        if k not in self.sem:
            self.mksem(k)
        q.dma_start(out=out, in_=in_, **kw).then_inc(self.sem[k], 16)
        self.cnt[k] += 16
        return (k, self.cnt[k])

    def wait2(self, eng, tok):
        if tok is None:
            return
        k, v = tok
        if v > 0:
            eng.wait_ge(self.sem[k], v)

    def final_dma_waits(self, eng):
        for k in sorted(self.sem):
            if k.startswith("D") and self.cnt[k] > 0:
                eng.wait_ge(self.sem[k], self.cnt[k])

    def inc(self, inst, sem):
        inst.then_inc(self.sem[sem], 1)
        self.cnt[sem] += 1
        return self.cnt[sem]

    def wait(self, eng, sem, tok):
        if tok > 0:
            eng.wait_ge(self.sem[sem], tok)

    def fence(self, eng, sem):
        tok = self.inc(eng.drain(), sem)
        eng.wait_ge(self.sem[sem], tok)


def build_nc(debug=False, phases=6):
    nc = bass.Bass("TRN2", target_bir_lowering=False, detect_race_conditions=False)

    def DI(name, shape, dt=F32):
        return nc.dram_tensor(name, list(shape), dt, kind="ExternalInput").ap()

    ipw_cols = sum(((NK * cb + RUN - 1) // RUN) * RUNP for cb in IP_BLK)
    xh = DI("xh", (XROWS, HID))
    xT = DI("xT", (HID, T))
    w_ip = DI("w_ip", (128, ipw_cols))
    w_fb = DI("w_fb", (128, 16 * 4 * RUNP))
    w_wo = DI("w_wo", (128, 32 * 4 * RUNP))
    w_al = DI("w_al", (128, 2 * RUNP))
    w_bl = DI("w_bl", (128, 2 * RUNP))
    w_rt = DI("w_rt", (128, 1 * RUNP))
    c_convw = DI("c_convw", (128, 320))
    c_convb = DI("c_convb", (128, 80))
    c_dtb = DI("c_dtb", (128, 1))
    c_alog = DI("c_alog", (128, 1))
    c_dcol = DI("c_dcol", (128, 64))
    c_drep = DI("c_drep", (128, 4 * RUNP))
    c_nw = DI("c_nw", (128, 64))
    c_bb = DI("c_bb", (1, 2 * RUNP))
    c_id = DI("c_id", (128, 128))
    c_tri = DI("c_tri", (128, 128))
    c_on3 = DI("c_on3", (3, 128))
    c_on1 = DI("c_on1", (1, 512))
    c_em = DI("c_em", (1, 1024))
    c_sc = DI("c_sc", (128, 2))
    c_flag = DI("c_flag", (1, 1))

    outT = nc.dram_tensor("outT", [HID, T], F32, kind="ExternalOutput").ap()

    d_hsbc = nc.dram_tensor("d_hsbc", [80, 128, 1032], BF16, kind="Internal").ap()
    d_sg = nc.dram_tensor("d_sg", [NZ, 128, 8224], BF16, kind="Internal").ap()
    d_xch = nc.dram_tensor("d_xch", [NZ, 128, INTER], BF16, kind="Internal").ap()
    d_s = nc.dram_tensor("d_s", [NZ, 128, INTER], BF16, kind="Internal").ap()
    d_gffn = nc.dram_tensor("d_gffn", [HID, 1032], BF16, kind="Internal").ap()
    d_ggt = nc.dram_tensor("d_ggt", [64, 128, 1032], BF16, kind="Internal").ap()
    d_comb = nc.dram_tensor("d_comb", [8, 1024], BF16, kind="Internal").ap()
    d_ccin = nc.dram_tensor("d_ccin", [128, INTER], BF16, kind="Internal").ap()
    d_ccout = nc.dram_tensor("d_ccout", [256, INTER], BF16, kind="Internal").ap()

    dbg = {}
    if debug:
        def DO(name, shape, dt=F32):
            return nc.dram_tensor(name, list(shape), dt, kind="ExternalOutput").ap()
        dbg["xnt"] = DO("dbg_xnt", (128, 32 * 1028))
        dbg["hsbc"] = DO("dbg_hsbc", (80, 128, 1032), BF16)
        dbg["sg"] = DO("dbg_sg", (NZ, 128, 8224), BF16)
        dbg["dt"] = DO("dbg_dt", (128, 1024))
        dbg["acum"] = DO("dbg_acum", (128, 1024))
        dbg["xch"] = DO("dbg_xch", (NZ, 128, INTER), BF16)
        dbg["s"] = DO("dbg_s", (NZ, 128, INTER), BF16)
        dbg["prev"] = DO("dbg_prev", (128, INTER))
        dbg["gffn"] = DO("dbg_gffn", (HID, 1032), BF16)
        dbg["ggt"] = DO("dbg_ggt", (64, 128, 1032), BF16)
        dbg["yz"] = DO("dbg_yz", (128, INTER))
        dbg["gtm"] = DO("dbg_gtm", (128, 1024))
        dbg["ylast"] = DO("dbg_ylast", (128, INTER), BF16)
        dbg["expa"] = DO("dbg_expa", (128, 128))

    es = ExitStack()

    def SB(name, shape, dtype):
        return es.enter_context(nc.sbuf_tensor(name, list(shape), dtype)).ap()

    ar1 = nc.alloc_sbuf_tensor("arena1", [128, 131072], mybir.dt.uint8)
    ar2 = nc.alloc_sbuf_tensor("arena2", [128, 49152], mybir.dt.uint8)
    a1 = nc.lookup_mloc(ar1).addr
    a2 = nc.lookup_mloc(ar2).addr

    def carve(base, name, shape, dtype, off):
        h = nc.virtual_sbuf_tensor(name, list(shape), dtype)
        nc._tensor(h.name, list(shape), dtype, type="SB")
        ml = nc.lookup_mloc(h)
        ml.allocated = True
        ml.addr = base + off
        return h.ap()

    xnT = carve(a1, "xnT", (128, 32 * 1028), BF16, 0)
    xzb = carve(a1, "xzb", (128, INTER), BF16, 65792)
    szb = carve(a1, "szb", (128, INTER), BF16, 82176)
    hszh = carve(a1, "hszh", (128, 4096), BF16, 98560)
    bzb = carve(a1, "bzb", (128, 1024), BF16, 106752)
    btm = carve(a1, "btm", (128, 256), BF16, 108800)
    xch2 = carve(a1, "xch2", (128, 1024), BF16, 109312)
    sgbuf = carve(a1, "sgbuf", (128, 2048), BF16, 109824)
    sgTs = carve(a1, "sgTs", (128, 2048), BF16, 113920)
    hbuf = carve(a1, "hbuf", (128, 2048), BF16, 118016)
    cbuf = carve(a1, "cbuf", (128, 1024), F32, 122112)
    sqdump = carve(a1, "sqdump", (128, HID), BF16, 122112)
    bfl = carve(a1, "bfl", (128, HID), BF16, 65792)
    hwT = carve(a1, "hwT", (128, 1024), BF16, 82176)
    sg_z = carve(a1, "sg_z", (128, INTER), BF16, 0)
    xch_z = carve(a1, "xch_z", (128, INTER), BF16, 16384)
    s_z = carve(a1, "s_z", (128, INTER), BF16, 32768)
    hs_z = carve(a1, "hs_z", (128, INTER), BF16, 49152)
    bz_c = carve(a1, "bz_c", (128, 1024), BF16, 65536)
    cz_c = carve(a1, "cz_c", (128, 1024), BF16, 67584)
    y_z = carve(a1, "y_z", (128, INTER), BF16, 69632)
    pvb = carve(a1, "pvb", (128, 512), BF16, 86016)
    xinm = carve(a1, "xinm", (128, INTER), BF16, 102912)
    prevC = carve(a1, "prevC", (128, INTER), F32, 87040)
    ggTs = carve(a1, "ggTs", (128, INTER), BF16, 32768)
    csc = carve(a1, "csc", (128, 2048), BF16, 123392)
    ggT = carve(a1, "ggT", (128, 65536), BF16, 0)

    xa0 = carve(a2, "xa0", (128, HID), F32, 0)
    xa1 = carve(a2, "xa1", (128, HID), F32, 16384)
    xnb = carve(a2, "xnb", (128, HID), BF16, 32768)
    wsb0 = carve(a2, "wsb0", (128, 12288), BF16, 0)
    wsb1 = carve(a2, "wsb1", (128, 12288), BF16, 24576)
    prev = carve(a2, "prev", (128, INTER), F32, 0)
    acT = carve(a2, "acT", (128, 128), F32, 32768)
    spl3 = carve(a2, "spl3", (128, 384), BF16, 33280)
    splt = carve(a2, "splt", (128, 256), F32, 34048)
    r3 = carve(a2, "r3", (3, 128), BF16, 35072)
    az3 = carve(a2, "az3", (128, 4), BF16, 35328)
    eazb = carve(a2, "eazb", (128, 128), F32, 35360)
    dec_zT = carve(a2, "dec_zT", (128, 128), F32, 35872)
    ddz = carve(a2, "ddz", (128, 128), F32, 36384)
    wrk = carve(a2, "wrk", (128, 128), F32, 36896)
    wrk2 = carve(a2, "wrk2", (128, 128), F32, 37408)
    gtm = carve(a2, "gtm", (128, 1024), F32, 37920)
    expA = carve(a2, "expA", (128, 128), F32, 42016)
    ltw = carve(a2, "ltw", (128, 128), F32, 42528)
    splTI = carve(a2, "splTI", (128, 384), BF16, 43040)
    azcol = carve(a2, "azcol", (128, 2), F32, 43808)
    ltw8 = carve(a2, "ltw8", (128, 1024), F32, 0)
    mtb8 = carve(a2, "mtb8", (128, 1024), BF16, 4096)
    wrk8 = carve(a2, "wrk8", (128, 512), F32, 6144)
    r3b8 = carve(a2, "r3b8", (3, 1024), BF16, 8192)
    drep = carve(a2, "drep", (128, INTER), BF16, 10240)
    wfb0 = carve(a2, "wfb0", (128, 8192), BF16, 0)
    wfb1 = carve(a2, "wfb1", (128, 8192), BF16, 16384)
    crep = carve(a2, "crep", (128, 1024), F32, 44352)
    combrow = carve(a2, "combrow", (1, 1024), BF16, 32768)
    bbasebf = carve(a2, "bbasebf", (1, HID), BF16, 34816)

    wdb0 = carve(a2, "wdb0", (128, 8192), BF16, 0)
    wdb1 = carve(a2, "wdb1", (128, 8192), BF16, 16384)
    otb = carve(a2, "otb", (128, 1024), F32, 32768)
    gft = carve(a2, "gft", (128, 1024), BF16, 36864)
    xtt = carve(a2, "xtt", (128, 1024), F32, 38912)

    dtTb = SB("dtTb", (128, 1024), F32)
    acum = SB("acum", (128, 1024), F32)
    dtcm = SB("dtcm", (128, 1024), F32)
    acolh = SB("acolh", (128, 1), F32)
    k_id = SB("k_id", (128, 128), BF16)
    k_tri = SB("k_tri", (128, 128), F32)
    k_on3 = SB("k_on3", (3, 128), BF16)
    k_on1 = SB("k_on1", (1, 512), BF16)
    k_em = SB("k_em", (1, 1024), BF16)
    k_convw = SB("k_convw", (128, 320), F32)
    k_convb = SB("k_convb", (128, 80), F32)
    k_dtb = SB("k_dtb", (128, 1), F32)
    k_alog = SB("k_alog", (128, 1), F32)
    k_dcol = SB("k_dcol", (128, 64), F32)
    k_nw = SB("k_nw", (128, 64), F32)
    k_flag = SB("k_flag", (1, 1), F32)
    k_sc = SB("k_sc", (128, 2), F32)
    k_flagb = SB("k_flagb", (1, 1), BF16)
    flagc = SB("flagc", (128, 1), F32)
    lncol = SB("lncol", (128, 4), F32)
    tpw = SB("tpw", (128, 64), F32)
    tpc = SB("tpc", (128, 16), F32)
    comb = SB("comb", (128, 64), F32)
    combb = SB("combb", (128, 64), BF16)
    dbuf = SB("dbuf", (128, 1024), F32)
    tbuf = SB("tbuf", (128, 1024), F32)
    sumg = SB("sumg", (128, 8), F32)
    rstd = SB("rstd", (128, 8), F32)
    eaz1 = SB("eaz1", (1, 128), F32)
    eaz2 = SB("eaz2", (1, 256), BF16)

    pp0 = es.enter_context(nc.psum_tensor("pp0", [128, 1536], F32))
    pp1 = es.enter_context(nc.psum_tensor("pp1", [128, 1536], F32))
    pbA = es.enter_context(nc.psum_tensor("pbA", [128, 1024], BF16))
    pbB = es.enter_context(nc.psum_tensor("pbB", [128, 1024], BF16))
    P0, P1, QA, QB = pp0.ap(), pp1.ap(), pbA.ap(), pbB.ap()
    BK = [P0[:, 0:512], P0[:, 512:1024], P0[:, 1024:1536],
          P1[:, 0:512], P1[:, 512:1024], P1[:, 1024:1536]]

    c = Ctx(nc)
    for s in ("LD", "WD", "PE", "VE", "SE", "CC", "ST"):
        c.mksem(s)
    V, S_, TE, SY, GP = nc.vector, nc.scalar, nc.tensor, nc.sync, nc.gpsimd

    blk = es.enter_context(nc.Block())

    c.dma(SY, k_tri[:], c_tri, "KC")
    c.dma(SY, k_convw[:], c_convw, "KC")
    c.dma(SY, k_convb[:], c_convb, "KC")
    c.dma(SY, k_dtb[:], c_dtb, "KC")
    c.dma(SY, k_alog[:], c_alog, "KC")
    c.dma(SY, k_dcol[:], c_dcol, "KC")
    c.dma(SY, k_nw[:], c_nw, "KC")
    c.dma(SY, k_flag[:], c_flag, "KC")
    t_fl = c.dma(SY, k_sc[:], c_sc, "KC")
    c.dma(GP, k_id[:], c_id, "KW")
    c.dma(GP, k_on3[:], c_on3, "KW")
    c.dma(GP, k_on1[:], c_on1, "KW")
    t_km = c.dma(GP, k_em[:], c_em, "KW")

    c.wait2(S_, t_fl)
    S_.activation(acolh[:], k_alog[:], AF.Exp)
    t0 = c.inc(S_.drain(), "SE")
    c.wait(V, "SE", t0)
    V.tensor_scalar_mul(acolh[:], acolh[:], -1.0)
    t_acol = c.inc(V.drain(), "VE")
    c.wait2(V, t_fl)
    V.tensor_copy(k_flagb[:], k_flag[:])
    t_flb = c.inc(V.drain(), "VE")
    c.wait2(TE, t_km)
    c.wait(TE, "VE", t_flb)
    TE.matmul(BK[0][:, 0:1], k_on1[0:1, 0:128], k_flagb[:], start=True, stop=True)
    t1 = c.inc(TE.drain(), "PE")
    c.wait(V, "PE", t1)
    V.tensor_copy(flagc[:], BK[0][:, 0:1])
    t_flagc = c.inc(V.drain(), "VE")

    xa = [xa0, xa1]
    t_xd = [None] * 9
    t_cons = [0] * 9
    t_trd = [0] * 9
    t_evq = [0, 0]
    qpar = 0
    t_xd[0] = c.dma(SY, xa[0][:], xh[0:128, :], "X0")
    t_xd[1] = c.dma(SY, xa[1][:], xh[128:256, :], "X1")
    for i in range(9):
        b = xa[i % 2]
        c.wait2(V, t_xd[i])
        V.reduce_sum(lncol[:, 0:1], b[:], axis=AXX)
        c.fence(V, "VE")
        V.tensor_scalar_mul(lncol[:, 1:2], lncol[:, 0:1], 1.0 / HID)
        c.fence(V, "VE")
        V.tensor_scalar(b[:], b[:], lncol[:, 1:2], None, ALU.subtract)
        t_sub = c.inc(V.drain(), "VE")
        c.wait(S_, "VE", t_sub)
        S_.activation(sqdump[:], b[:], AF.Square)
        t_sq = c.inc(S_.drain(), "SE")
        c.wait(V, "SE", t_sq)
        V.reduce_sum(lncol[:, 2:3], sqdump[:], axis=AXX)
        t_v2 = c.inc(V.drain(), "VE")
        c.wait(S_, "VE", t_v2)
        S_.activation(lncol[:, 3:4], lncol[:, 2:3], AF.Ln, bias=k_sc[:, 0:1], scale=1.0 / HID)
        c.fence(S_, "SE")
        S_.activation(lncol[:, 2:3], lncol[:, 3:4], AF.Exp, scale=-0.5)
        t_rs = c.inc(S_.drain(), "SE")
        c.wait(V, "SE", t_rs)
        if i >= 2:
            c.wait(V, "PE", t_trd[i - 2])
        if i >= 1:
            c.wait(V, "PE", t_trd[i - 1])
        V.tensor_scalar(xnb[:], b[:], lncol[:, 2:3], None, ALU.mult)
        t_xn = c.inc(V.drain(), "VE")
        t_cons[i] = t_xn
        if i + 2 < 9:
            c.wait(SY, "VE", t_xn)
            t_xd[i + 2] = c.dma(SY, xa[i % 2][:], xh[(i + 2) * 128:(i + 3) * 128, :], f"X{i % 2}")
        ncols = 128 if i < 8 else 3
        c.wait(TE, "VE", t_xn)
        c.wait2(TE, t_km)
        for r in range(4):
            q = [QA, QB][qpar]
            c.wait(TE, "SE", t_evq[qpar])
            for kk in range(8):
                k = r * 8 + kk
                TE.transpose(q[:, kk * 128:kk * 128 + 128], xnb[:, k * 128:(k + 1) * 128], k_id[:])
            t_t = c.inc(TE.drain(), "PE")
            c.wait(S_, "PE", t_t)
            for kk in range(8):
                k = r * 8 + kk
                S_.copy(xnT[:, k * 1028 + i * 128: k * 1028 + i * 128 + ncols],
                        q[:, kk * 128:kk * 128 + ncols])
            t_evq[qpar] = c.inc(S_.drain(), "SE")
            qpar ^= 1
        t_trd[i] = c.cnt["PE"]
    t_s0_done = c.inc(S_.drain(), "SE")
    if debug:
        c.wait(V, "SE", t_s0_done)
    if phases == 1:
        c.wait(SY, "SE", t_s0_done)
        c.dma(SY, outT, xT, "OUT")
        c.final_dma_waits(SY)
        es.close()
        return nc

    wsb = [wsb0, wsb1]
    ipw_off = []
    off = 0
    for cb in IP_BLK:
        nr = (NK * cb + RUN - 1) // RUN
        ipw_off.append((off, nr))
        off += nr * RUNP
    t_wip = [None] * 49
    t_blk_mm = [0] * 49
    t_chunk_ev = {}
    c.wait(GP, "PE", c.cnt["PE"])
    c.wait(GP, "VE", c.cnt["VE"])
    c.wait(GP, "SE", c.cnt["SE"])
    for b in range(min(2, 49)):
        o, nr = ipw_off[b]
        src = w_ip[:, o:o + nr * RUNP].rearrange("p (r q) -> p r q", q=RUNP)[:, :, 0:RUN]
        t_wip[b] = c.dma(GP, wsb[b % 2][:, 0:nr * RUN], src, f"W{b % 2}")
    t_pp_ev = [None, None]
    t_pp_mm = [0, 0]
    t_hs_dma = [None, None]
    t_sg_dma = [None, None]
    t_sgb = [0, 0]
    t_sgts = [0, 0]
    cpar = 0
    gpar = 0
    chunk_id = 0
    t_dtT = 0
    t_scan = 0
    for b in range(49):
        o, nr = ipw_off[b]
        cb = IP_BLK[b]
        nchk = cb // 128
        c.wait2(TE, t_wip[b])
        for j in range(nchk):
            cg = chunk_id
            pp = [P0, P1][cpar]
            if t_pp_ev[cpar] is not None:
                sem_n, tok_n = t_pp_ev[cpar]
                c.wait(TE, sem_n, tok_n)
            for k in range(NK):
                st = wsb[b % 2][:, k * cb + j * 128: k * cb + j * 128 + 128]
                TE.matmul(pp[:, 0:512], st, xnT[:, k * 1028:k * 1028 + 512],
                          start=(k == 0), stop=(k == NK - 1))
                TE.matmul(pp[:, 512:1024], st, xnT[:, k * 1028 + 512:k * 1028 + 1024],
                          start=(k == 0), stop=(k == NK - 1))
                TE.matmul(pp[:, 1024:1027], st, xnT[:, k * 1028 + 1024:k * 1028 + 1027],
                          start=(k == 0), stop=(k == NK - 1))
            t_pp_mm[cpar] = c.inc(TE.drain(), "PE")
            if cg < 64:
                c.wait(S_, "PE", t_pp_mm[cpar])
                c.wait(S_, "PE", t_sgb[gpar])
                S_.activation(sgbuf[:, gpar * 1024:(gpar + 1) * 1024], pp[:, 3:1027], AF.Silu)
                t_si = c.inc(S_.drain(), "SE")
                t_pp_ev[cpar] = ("SE", t_si)
                c.wait(TE, "SE", t_si)
                c.wait(TE, "SE", t_sgts[gpar])
                c.wait2(TE, t_sg_dma[gpar])
                q = [QA, QB][gpar]
                for z in range(8):
                    TE.transpose(q[:, z * 128:(z + 1) * 128],
                                 sgbuf[:, gpar * 1024 + z * 128: gpar * 1024 + (z + 1) * 128], k_id[:])
                t_sgb[gpar] = c.inc(TE.drain(), "PE")
                c.wait(S_, "PE", t_sgb[gpar])
                S_.copy(sgTs[:, gpar * 1024:(gpar + 1) * 1024], q[:, 0:1024])
                t_sgts[gpar] = c.inc(S_.drain(), "SE")
                c.wait(SY, "SE", t_sgts[gpar])
                src = sgTs[:, gpar * 1024:(gpar + 1) * 1024].rearrange("p (z ch) -> p z ch", z=8)
                dst = d_sg[:, :, cg * 128:(cg + 1) * 128].rearrange("z p ch -> p z ch")
                t_sg_dma[gpar] = c.dma(SY, dst, src, f"SG{gpar}")
                gpar ^= 1
            elif cg < 144:
                cc = cg - 64
                c.wait(V, "PE", t_pp_mm[cpar])
                V.tensor_scalar(cbuf[:], pp[:, 0:1024], k_convw[:, cc * 4:cc * 4 + 1], None, ALU.mult)
                V.scalar_tensor_tensor(cbuf[:], pp[:, 1:1025], k_convw[:, cc * 4 + 1:cc * 4 + 2], cbuf[:], ALU.mult, ALU.add)
                V.scalar_tensor_tensor(cbuf[:], pp[:, 2:1026], k_convw[:, cc * 4 + 2:cc * 4 + 3], cbuf[:], ALU.mult, ALU.add)
                V.scalar_tensor_tensor(cbuf[:], pp[:, 3:1027], k_convw[:, cc * 4 + 3:cc * 4 + 4], cbuf[:], ALU.mult, ALU.add)
                t_cv = c.inc(V.drain(), "VE")
                t_pp_ev[cpar] = ("VE", t_cv)
                hpar = cc % 2
                c.wait(S_, "VE", t_cv)
                c.wait2(S_, t_hs_dma[hpar])
                S_.activation(hbuf[:, hpar * 1024:(hpar + 1) * 1024], cbuf[:], AF.Silu,
                              bias=k_convb[:, cc:cc + 1])
                t_hb = c.inc(S_.drain(), "SE")
                c.wait(SY, "SE", t_hb)
                t_hs_dma[hpar] = c.dma(SY, d_hsbc[cc, :, 0:1024],
                                       hbuf[:, hpar * 1024:(hpar + 1) * 1024], f"HB{hpar}")
            else:
                c.wait(V, "PE", t_pp_mm[cpar])
                V.tensor_scalar(dbuf[:], pp[:, 3:1027], k_dtb[:], None, ALU.add)
                t_d0 = c.inc(V.drain(), "VE")
                t_pp_ev[cpar] = ("VE", t_d0)
                c.wait(S_, "VE", t_d0)
                S_.activation(tbuf[:], dbuf[:], AF.Abs)
                c.fence(S_, "SE")
                S_.activation(tbuf[:], tbuf[:], AF.Exp, scale=-1.0)
                c.fence(S_, "SE")
                S_.activation(tbuf[:], tbuf[:], AF.Ln, bias=k_sc[:, 1:2])
                t_d1 = c.inc(S_.drain(), "SE")
                c.wait(V, "SE", t_d1)
                V.tensor_scalar_max(dtcm[:], dbuf[:], 0.0)
                c.fence(V, "VE")
                V.tensor_add(dtcm[:], dtcm[:], tbuf[:])
                c.fence(V, "VE")
                V.tensor_scalar_min(dtcm[:], dtcm[:], 100.0)
                c.fence(V, "VE")
                V.tensor_copy(sgbuf[:, 0:1024], dtcm[:])
                t_dtc = c.inc(V.drain(), "VE")
                c.wait(TE, "VE", t_dtc)
                c.wait(TE, "SE", c.cnt["SE"])
                for z in range(8):
                    TE.transpose(QA[:, z * 128:(z + 1) * 128],
                                 sgbuf[:, z * 128:(z + 1) * 128], k_id[:])
                t_tz = c.inc(TE.drain(), "PE")
                c.wait(S_, "PE", t_tz)
                S_.copy(dtTb[:], QA[:, 0:1024])
                t_dtT = c.inc(S_.drain(), "SE")
                c.wait(V, "VE", t_acol)
                V.tensor_scalar(dtcm[:], dtcm[:], acolh[:], None, ALU.mult)
                c.fence(V, "VE")
                for z in range(8):
                    V.tensor_tensor_scan(acum[:, z * 128:(z + 1) * 128],
                                         dtcm[:, z * 128:(z + 1) * 128],
                                         dtcm[:, z * 128:(z + 1) * 128],
                                         0.0, ALU.add, ALU.bypass)
                t_scan = c.inc(V.drain(), "VE")
            chunk_id += 1
            cpar ^= 1
        t_blk_mm[b] = c.cnt["PE"]
        if b + 2 < 49:
            o2, nr2 = ipw_off[b + 2]
            c.wait(GP, "PE", t_blk_mm[b])
            src = w_ip[:, o2:o2 + nr2 * RUNP].rearrange("p (r q) -> p r q", q=RUNP)[:, :, 0:RUN]
            t_wip[b + 2] = c.dma(GP, wsb[b % 2][:, 0:nr2 * RUN], src, f"W{b % 2}")

    if debug:
        c.wait(SY, "VE", t_scan)
        c.wait(SY, "SE", t_dtT)
        c.wait2(SY, t_sg_dma[0])
        c.wait2(SY, t_sg_dma[1])
        c.wait2(SY, t_hs_dma[0])
        c.wait2(SY, t_hs_dma[1])
        c.dma(SY, dbg["dt"], dtcm[:], "DBG")
        c.dma(SY, dbg["acum"], acum[:], "DBG")
        c.dma(SY, dbg["hsbc"][:, :, 0:1024], d_hsbc[:, :, 0:1024], "DBG")
        c.dma(SY, dbg["sg"][:, :, 0:8192], d_sg[:, :, 0:8192], "DBG")
    if phases == 2:
        c.dma(SY, outT, xT, "OUT")
        c.final_dma_waits(SY)
        es.close()
        return nc

    def emit_ablock(z, want_dec, want_expA, want_splTI):
        acz = acum[:, z * 128:(z + 1) * 128]
        c.wait(V, "VE", t_scan)
        c.wait(TE, "VE", c.cnt["VE"])
        V.tensor_copy(spl3[:, 0:128], acz)
        V.tensor_tensor(splt[:, 0:128], acz, spl3[:, 0:128], ALU.subtract)
        V.tensor_copy(spl3[:, 128:256], splt[:, 0:128])
        V.tensor_tensor(splt[:, 128:256], splt[:, 0:128], spl3[:, 128:256], ALU.subtract)
        V.tensor_copy(spl3[:, 256:384], splt[:, 128:256])
        t_sp = c.inc(V.drain(), "VE")
        c.wait(TE, "VE", t_sp)
        for kk in range(3):
            TE.transpose(QA[:, kk * 128:(kk + 1) * 128], spl3[:, kk * 128:(kk + 1) * 128], k_id[:])
        t_tr = c.inc(TE.drain(), "PE")
        c.wait(V, "PE", t_tr)
        V.tensor_copy(acT[:], QA[:, 0:128])
        V.tensor_tensor(acT[:], acT[:], QA[:, 128:256], ALU.add)
        V.tensor_tensor(acT[:], acT[:], QA[:, 256:384], ALU.add)
        if want_splTI:
            V.tensor_copy(splTI[:, 0:128], acT[:])
            V.tensor_tensor(splt[:, 0:128], acT[:], splTI[:, 0:128], ALU.subtract)
            V.tensor_copy(splTI[:, 128:256], splt[:, 0:128])
            V.tensor_tensor(splt[:, 128:256], splt[:, 0:128], splTI[:, 128:256], ALU.subtract)
            V.tensor_copy(splTI[:, 256:384], splt[:, 128:256])
        t_acT = c.inc(V.drain(), "VE")
        V.tensor_copy(az3[:, 0:1], acz[:, 127:128])
        c.fence(V, "VE")
        V.tensor_tensor(azcol[:, 0:1], acz[:, 127:128], az3[:, 0:1], ALU.subtract)
        c.fence(V, "VE")
        V.tensor_copy(az3[:, 1:2], azcol[:, 0:1])
        c.fence(V, "VE")
        V.tensor_tensor(azcol[:, 1:2], azcol[:, 0:1], az3[:, 1:2], ALU.subtract)
        c.fence(V, "VE")
        V.tensor_copy(az3[:, 2:3], azcol[:, 1:2])
        t_az = c.inc(V.drain(), "VE")
        c.wait(TE, "VE", t_az)
        TE.transpose(QB[0:3, 0:128], az3[:, 0:3], k_id[:])
        t_azt = c.inc(TE.drain(), "PE")
        c.wait(V, "PE", t_azt)
        V.tensor_copy(r3[:], QB[0:3, 0:128])
        t_r3 = c.inc(V.drain(), "VE")
        c.wait(TE, "VE", t_r3)
        TE.matmul(BK[4][:, 0:128], k_on3[:], r3[:], start=True, stop=True)
        t_azbc = c.inc(TE.drain(), "PE")
        c.wait(V, "PE", t_azbc)
        V.tensor_scalar_max(wrk[0:1, 0:128], BK[4][0:1, 0:128], -80.0)
        t_eazc = c.inc(V.drain(), "VE")
        c.wait(S_, "VE", t_eazc)
        S_.activation(eaz1[:], wrk[0:1, 0:128], AF.Exp)
        t_e1 = c.inc(S_.drain(), "SE")
        c.wait(V, "SE", t_e1)
        V.tensor_copy(eaz2[0:1, 0:128], eaz1[:])
        c.fence(V, "VE")
        V.tensor_tensor(wrk[0:1, 0:128], eaz1[:], eaz2[0:1, 0:128], ALU.subtract)
        c.fence(V, "VE")
        V.tensor_copy(eaz2[0:1, 128:256], wrk[0:1, 0:128])
        t_e2 = c.inc(V.drain(), "VE")
        c.wait(TE, "VE", t_e2)
        TE.matmul(BK[5][:, 0:128], k_on1[0:1, 0:128], eaz2[0:1, 0:128], start=True, stop=False)
        TE.matmul(BK[5][:, 0:128], k_on1[0:1, 0:128], eaz2[0:1, 128:256], start=False, stop=True)
        t_ebc = c.inc(TE.drain(), "PE")
        c.wait(V, "PE", t_ebc)
        c.wait(V, "VE", t_acT)
        V.tensor_copy(eazb[:], BK[5][:, 0:128])
        if want_dec:
            V.tensor_tensor(wrk2[:], BK[4][:, 0:128], acT[:], ALU.subtract)
            V.tensor_scalar_max(wrk2[:], wrk2[:], -80.0)
            t_w = c.inc(V.drain(), "VE")
            c.wait(S_, "VE", t_w)
            S_.activation(dec_zT[:], wrk2[:], AF.Exp)
        if want_expA:
            V.tensor_scalar_max(ltw[:], acT[:], -80.0)
            t_w2 = c.inc(V.drain(), "VE")
            c.wait(S_, "VE", t_w2)
            S_.activation(expA[:], ltw[:], AF.Exp)
        t_se = c.inc(S_.drain(), "SE")
        t_ve = c.inc(V.drain(), "VE")
        return t_se, t_ve

    V.memset(prev[:], 0.0)
    t_pm = c.inc(V.drain(), "VE")
    c.wait(V, "VE", t_pm)
    t_xs = [None] * (NZ + 1)
    t_ss = [None] * (NZ + 1)
    for z in range(NZ):
        c.wait2(SY, t_xs[z])
        c.wait2(SY, t_ss[z])
        c.wait(SY, "PE", c.cnt["PE"])
        srcB = d_hsbc[64:72, :, z * 128:(z + 1) * 128]
        t_bz = c.dma(SY, bzb[:].rearrange("p (k t) -> p k t", k=8),
                     srcB.rearrange("k p t -> p k t"), "BZ")
        srcH0 = d_hsbc[0:32, :, z * 128:(z + 1) * 128]
        t_h0 = c.dma(SY, hszh[:].rearrange("p (k t) -> p k t", k=32),
                     srcH0.rearrange("k p t -> p k t"), "HS")
        t_ab = emit_ablock(z, want_dec=True, want_expA=False, want_splTI=False)
        c.wait(V, "SE", t_ab[0])
        c.wait(V, "SE", t_dtT)
        V.tensor_tensor(ddz[:], dtTb[:, z * 128:(z + 1) * 128], dec_zT[:], ALU.mult)
        t_ddz = c.inc(V.drain(), "VE")
        t_qfree = [t_ddz, t_ddz]
        t_sfree = [t_ddz, t_ddz]
        t_btm = [0, 0]
        for sb in range(16):
            g = sb // 2
            par = sb % 2
            h0 = 8 * sb
            QX = [QA, QB][par]
            SBK = [BK[0], BK[1]][par]
            if sb == 8:
                c.wait(SY, "PE", c.cnt["PE"])
                srcH1 = d_hsbc[32:64, :, z * 128:(z + 1) * 128]
                t_h0 = c.dma(SY, hszh[:].rearrange("p (k t) -> p k t", k=32),
                             srcH1.rearrange("k p t -> p k t"), "HS")
            if sb % 2 == 0:
                gp = (sb // 2) % 2
                c.wait2(TE, t_bz)
                c.wait(TE, "VE", t_btm[gp])
                c.wait(TE, "VE", t_qfree[1])
                TE.transpose(QB[:, 512:640], bzb[:, g * 128:(g + 1) * 128], k_id[:])
                t_bq = c.inc(TE.drain(), "PE")
                c.wait(V, "PE", t_bq)
                V.tensor_copy(btm[:, gp * 128:(gp + 1) * 128], QB[:, 512:640])
                t_btm[gp] = c.inc(V.drain(), "VE")
            c.wait2(TE, t_h0)
            c.wait(TE, "VE", t_qfree[par])
            c.wait(TE, "VE", t_btm[(sb // 2) % 2])
            for q in range(4):
                TE.transpose(QX[:, q * 128:(q + 1) * 128],
                             hszh[:, ((4 * sb + q) % 32) * 128:((4 * sb + q) % 32 + 1) * 128],
                             k_id[:])
            t_pt = c.inc(TE.drain(), "PE")
            c.wait(V, "PE", t_pt)
            c.wait(V, "VE", t_ddz)
            if sb == 0:
                c.wait2(V, t_xs[z])
                c.wait2(V, t_ss[z])
            for kk in range(8):
                h = h0 + kk
                V.tensor_scalar(xzb[:, h * 64:(h + 1) * 64], QX[:, kk * 64:(kk + 1) * 64],
                                dtTb[:, z * 128 + h:z * 128 + h + 1], None, ALU.mult)
            for kk in range(8):
                h = h0 + kk
                V.tensor_scalar(xch2[:, (par * 8 + kk) * 64:(par * 8 + kk + 1) * 64],
                                QX[:, kk * 64:(kk + 1) * 64],
                                ddz[:, h:h + 1], None, ALU.mult)
            t_x2 = c.inc(V.drain(), "VE")
            t_qfree[par] = t_x2
            c.wait(TE, "VE", t_x2)
            c.wait(TE, "VE", t_sfree[par])
            TE.matmul(SBK[:, 0:512], btm[:, ((sb // 2) % 2) * 128:((sb // 2) % 2 + 1) * 128],
                      xch2[:, par * 512:(par + 1) * 512], start=True, stop=True)
            t_sm = c.inc(TE.drain(), "PE")
            c.wait(V, "PE", t_sm)
            V.tensor_copy(szb[:, sb * 512:(sb + 1) * 512], SBK[:, 0:512])
            for kk in range(8):
                h = h0 + kk
                V.scalar_tensor_tensor(prev[:, h * 64:(h + 1) * 64],
                                       prev[:, h * 64:(h + 1) * 64],
                                       eazb[:, h:h + 1], SBK[:, kk * 64:(kk + 1) * 64],
                                       ALU.mult, ALU.add)
            t_sv = c.inc(V.drain(), "VE")
            t_sfree[par] = t_sv
        t_zv = c.inc(V.drain(), "VE")
        c.wait(SY, "VE", t_zv)
        t_xs[z + 1] = c.dma(SY, d_xch[z], xzb[:], "XS")
        t_ss[z + 1] = c.dma(SY, d_s[z], szb[:], "SS")
    c.wait2(V, t_xs[NZ])
    c.wait2(V, t_ss[NZ])
    V.tensor_copy(xzb[:], prev[:])
    t_xo = c.inc(V.drain(), "VE")
    c.wait(SY, "VE", t_xo)
    t_ci = c.dma(SY, d_ccin, xzb[:], "CI")
    c.wait2(GP, t_ci)
    GP.collective_compute("AllGather", ALU.bypass,
                          replica_groups=[[0, 4], [1, 5], [2, 6], [3, 7]],
                          ins=[d_ccin], outs=[d_ccout]).then_inc(c.sem["CC"], 1)
    c.cnt["CC"] += 1
    if debug:
        c.wait(SY, "VE", c.cnt["VE"])
        c.dma(SY, dbg["prev"], prev[:], "DBG")
        c.dma(SY, dbg["xch"], d_xch, "DBG")
        c.dma(SY, dbg["s"], d_s, "DBG")
    if phases == 3:
        c.dma(SY, outT, xT, "OUT")
        c.final_dma_waits(SY)
        es.close()
        return nc

    c.wait(GP, "PE", c.cnt["PE"])
    t_rt = c.dma(GP, btm[:, 0:256],
                 w_rt.rearrange("p (r q) -> p r q", q=RUNP)[:, 0:1, 0:256], "RT")
    t_ald = c.dma(GP, hszh[:],
                  w_al.rearrange("p (r q) -> p r q", q=RUNP)[:, :, 0:RUN], "AL")
    c.wait2(GP, t_ci)
    t_bfl = c.dma(GP, bfl[:],
                  w_bl.rearrange("p (r q) -> p r q", q=RUNP)[:, :, 0:RUN], "BL")
    t_bb0 = c.dma(GP, bbasebf[:],
                  c_bb.rearrange("p (r q) -> p r q", q=RUNP)[:, :, 0:RUN], "BB")
    t_rtev = 0
    for i in range(8):
        c.wait2(TE, t_rt)
        c.wait(TE, "VE", t_rtev)
        for k in range(NK):
            TE.matmul(BK[2][:, 0:8], xnT[:, k * 1028 + 3 + i * 128:k * 1028 + 3 + (i + 1) * 128],
                      btm[:, k * 8:(k + 1) * 8], start=(k == 0), stop=(k == NK - 1))
        t_rm = c.inc(TE.drain(), "PE")
        c.wait(V, "PE", t_rm)
        V.tensor_copy(tpw[:, i * 8:(i + 1) * 8], BK[2][:, 0:8])
        t_rtev = c.inc(V.drain(), "VE")
        c.fence(V, "VE")
        V.reduce_max(tpc[:, 0:1], tpw[:, i * 8:(i + 1) * 8], axis=AXX)
        c.fence(V, "VE")
        V.tensor_scalar_mul(tpc[:, 2:3], tpc[:, 0:1], -1.0)
        c.fence(V, "VE")
        V.tensor_scalar(tpc[:, 8:16], tpw[:, i * 8:(i + 1) * 8], tpc[:, 0:1], None, ALU.is_equal)
        c.fence(V, "VE")
        V.scalar_tensor_tensor(sumg[:], tpc[:, 8:16], -1e30, tpw[:, i * 8:(i + 1) * 8],
                               ALU.mult, ALU.add)
        c.fence(V, "VE")
        V.reduce_max(tpc[:, 1:2], sumg[:], axis=AXX)
        t_m2 = c.inc(V.drain(), "VE")
        c.wait(S_, "VE", t_m2)
        S_.activation(sumg[:], tpw[:, i * 8:(i + 1) * 8], AF.Exp, bias=tpc[:, 2:3])
        c.fence(S_, "SE")
        S_.activation(tpc[:, 3:4], tpc[:, 1:2], AF.Exp, bias=tpc[:, 2:3])
        t_se2 = c.inc(S_.drain(), "SE")
        c.wait(V, "SE", t_se2)
        V.tensor_scalar(tpc[:, 8:16], tpw[:, i * 8:(i + 1) * 8], tpc[:, 1:2], None, ALU.is_ge)
        c.fence(V, "VE")
        V.tensor_tensor(sumg[:], sumg[:], tpc[:, 8:16], ALU.mult)
        c.fence(V, "VE")
        V.tensor_scalar(tpc[:, 4:5], tpc[:, 3:4], 1.0, None, ALU.add)
        c.fence(V, "VE")
        V.reciprocal(tpc[:, 5:6], tpc[:, 4:5])
        c.fence(V, "VE")
        V.tensor_scalar(comb[:, i * 8:(i + 1) * 8], sumg[:], tpc[:, 5:6], 2.0,
                        ALU.mult, ALU.mult)
        c.fence(V, "VE")
    V.tensor_copy(combb[:], comb[:])
    t_cb = c.inc(V.drain(), "VE")
    c.wait(TE, "VE", t_cb)
    for i in range(8):
        TE.transpose(QA[0:8, i * 128:(i + 1) * 128], combb[:, i * 8:(i + 1) * 8], k_id[:])
    t_ct = c.inc(TE.drain(), "PE")
    c.wait(V, "PE", t_ct)
    V.tensor_copy(hwT[0:8, 0:1024], QA[0:8, 0:1024])
    t_cm = c.inc(V.drain(), "VE")
    c.wait(SY, "VE", t_cm)
    t_cd = c.dma(SY, d_comb, hwT[0:8, 0:1024], "CD")
    t_crmm = 0
    for e in range(8):
        c.wait2(SY, t_cd)
        c.wait(SY, "PE", t_crmm)
        t_cr = c.dma(SY, combrow[:], d_comb[e:e + 1, :], "CR")
        c.wait2(TE, t_cr)
        TE.matmul(BK[2][:, 0:512], k_em[0:1, e * 128:(e + 1) * 128],
                  combrow[0:1, 0:512], start=(e == 0), stop=(e == 7))
        TE.matmul(BK[3][:, 0:512], k_em[0:1, e * 128:(e + 1) * 128],
                  combrow[0:1, 512:1024], start=(e == 0), stop=(e == 7))
        t_crmm = c.inc(TE.drain(), "PE")
    t_crm = c.cnt["PE"]
    c.wait(V, "PE", t_crm)
    V.tensor_copy(crep[:, 0:512], BK[2][:, 0:512])
    V.tensor_copy(crep[:, 512:1024], BK[3][:, 0:512])
    t_crep = c.inc(V.drain(), "VE")
    t_hw = 0
    for tg in range(2):
        c.wait2(TE, t_ald)
        c.wait(TE, "VE", t_hw)
        for k in range(NK):
            TE.matmul(BK[4][:, 0:512], hszh[:, k * 128:(k + 1) * 128],
                      xnT[:, k * 1028 + 3 + tg * 512:k * 1028 + 3 + (tg + 1) * 512],
                      start=(k == 0), stop=(k == NK - 1))
        t_hm = c.inc(TE.drain(), "PE")
        c.wait(V, "PE", t_hm)
        c.wait(V, "VE", t_crep)
        c.wait2(V, t_cd)
        V.tensor_tensor(hwT[:, tg * 512:(tg + 1) * 512], BK[4][:, 0:512],
                        crep[:, tg * 512:(tg + 1) * 512], ALU.mult)
        t_hw = c.inc(V.drain(), "VE")
    t_hwd = c.inc(V.drain(), "VE")
    wfb = [wfb0, wfb1]
    t_wf = [None] * 16
    t_fbm = [0] * 16
    for ob in range(min(2, 16)):
        src = w_fb[:, ob * 4 * RUNP:(ob + 1) * 4 * RUNP].rearrange(
            "p (r q) -> p r q", q=RUNP)[:, :, 0:RUN]
        t_wf[ob] = c.dma(GP, wfb[ob % 2][:], src, f"WF{ob % 2}")
    t_gfev = [0, 0, 0, 0]
    t_gfdma = [None, None, None, None]
    for ob in range(16):
        c.wait2(TE, t_wf[ob])
        for o2 in range(2):
            o128 = ob * 2 + o2
            for tg in range(2):
                slot = (o2 * 2 + tg)
                pb_ = BK[slot]
                c.wait(TE, "SE", t_gfev[slot])
                for k in range(NK):
                    TE.matmul(pb_[:, 0:512],
                              wfb[ob % 2][:, k * 256 + o2 * 128:k * 256 + (o2 + 1) * 128],
                              xnT[:, k * 1028 + 3 + tg * 512:k * 1028 + 3 + (tg + 1) * 512],
                              start=(k == 0), stop=False)
                c.wait(TE, "VE", t_hwd)
                c.wait2(TE, t_bfl)
                TE.matmul(pb_[:, 0:512], bfl[:, o128 * 128:(o128 + 1) * 128],
                          hwT[:, tg * 512:(tg + 1) * 512], start=False, stop=False)
                c.wait2(TE, t_bb0)
                TE.matmul(pb_[:, 0:512], bbasebf[0:1, o128 * 128:(o128 + 1) * 128],
                          k_on1[0:1, 0:512], start=False, stop=True)
                t_m = c.inc(TE.drain(), "PE")
                c.wait(S_, "PE", t_m)
                c.wait2(S_, t_gfdma[slot])
                S_.activation(sgbuf[:, slot * 512:(slot + 1) * 512], pb_[:, 0:512], AF.Silu)
                t_gfev[slot] = c.inc(S_.drain(), "SE")
                c.wait(SY, "SE", t_gfev[slot])
                t_gfdma[slot] = c.dma(SY, d_gffn[o128 * 128:(o128 + 1) * 128,
                                                 tg * 512:(tg + 1) * 512],
                                      sgbuf[:, slot * 512:(slot + 1) * 512], f"GF{slot}")
        if ob + 2 < 16:
            c.wait(GP, "PE", c.cnt["PE"])
            src = w_fb[:, (ob + 2) * 4 * RUNP:(ob + 3) * 4 * RUNP].rearrange(
                "p (r q) -> p r q", q=RUNP)[:, :, 0:RUN]
            t_wf[ob + 2] = c.dma(GP, wfb[ob % 2][:], src, f"WF{ob % 2}")
    if debug:
        for sl in range(4):
            c.wait2(SY, t_gfdma[sl])
        c.dma(SY, dbg["gffn"][:, 0:1024], d_gffn[:, 0:1024], "DBG")
    if phases == 4:
        c.dma(SY, outT, xT, "OUT")
        c.final_dma_waits(SY)
        es.close()
        return nc

    c.wait(SY, "CC", 1)
    c.wait(SY, "PE", c.cnt["PE"])
    c.wait(SY, "SE", c.cnt["SE"])
    c.wait(GP, "PE", c.cnt["PE"])
    c.wait(GP, "VE", c.cnt["VE"])
    t_dr = c.dma(GP, drep[:],
                 c_drep.rearrange("p (r q) -> p r q", q=RUNP)[:, :, 0:RUN], "DR")
    t_xi = c.dma(SY, xinm[:], d_ccout[0:128, :], "XIN")
    c.wait2(V, t_xi)
    c.wait(V, "VE", t_flagc)
    c.wait(V, "PE", c.cnt["PE"])
    V.tensor_scalar(xinm[:], xinm[:], flagc[:], None, ALU.mult)
    t_xm = c.inc(V.drain(), "VE")
    c.wait(V, "VE", t_xm)
    V.tensor_copy(prevC[:], xinm[:])
    t_pv0 = c.inc(V.drain(), "VE")
    c.wait(V, "VE", t_pv0)
    t_zc = [None] * (NZ + 1)
    t_prevz = t_pv0
    t_zend = 0
    for z in range(NZ):
        c.wait(SY, "VE", t_zend)
        t_sgz = c.dma(SY, sg_z[:], d_sg[z, :, 0:8192], "SGZ")
        t_xcz = c.dma(SY, xch_z[:], d_xch[z], "XCZ")
        t_ssz = c.dma(SY, s_z[:], d_s[z], "SSZ")
        for q4 in range(4):
            t_hsz = c.dma(SY, hs_z[:, q4 * 2048:(q4 + 1) * 2048].rearrange(
                "p (k t) -> p k t", k=16),
                d_hsbc[q4 * 16:(q4 + 1) * 16, :, z * 128:(z + 1) * 128].rearrange(
                    "k p t -> p k t"), "HSZ")
        t_bzc = c.dma(SY, bz_c[:].rearrange("p (k t) -> p k t", k=8),
                      d_hsbc[64:72, :, z * 128:(z + 1) * 128].rearrange(
                          "k p t -> p k t"), "BZC")
        t_czc = c.dma(SY, cz_c[:].rearrange("p (k t) -> p k t", k=8),
                      d_hsbc[72:80, :, z * 128:(z + 1) * 128].rearrange(
                          "k p t -> p k t"), "CZC")
        t_ab = emit_ablock(z, want_dec=False, want_expA=True, want_splTI=True)
        c.wait2(V, t_hsz)
        if z == 0:
            c.wait2(V, t_dr)
        V.tensor_tensor(hs_z[:], hs_z[:], drep[:], ALU.mult)
        t_hsd = c.inc(V.drain(), "VE")
        t_gtev = [t_ab[1], t_ab[1]]
        for g in range(8):
            gb = [BK[1], BK[5]][g % 2]
            c.wait2(TE, t_bzc)
            c.wait2(TE, t_czc)
            c.wait(TE, "VE", t_gtev[g % 2])
            TE.matmul(gb[:, 0:128], bz_c[:, g * 128:(g + 1) * 128],
                      cz_c[:, g * 128:(g + 1) * 128], start=True, stop=True)
            t_g = c.inc(TE.drain(), "PE")
            c.wait(V, "PE", t_g)
            V.tensor_tensor(gtm[:, g * 128:(g + 1) * 128], gb[:, 0:128], k_tri[:], ALU.mult)
            t_gtev[g % 2] = c.inc(V.drain(), "VE")
        t_gtm = c.cnt["VE"]
        t_qfree = [t_gtm, t_gtm]
        t_y1free = [t_gtm, t_gtm]
        t_y2free = [t_gtm, t_gtm]
        t_lwprev = t_gtm
        t_y2pe = 0
        for sb in range(16):
            g = sb // 2
            par = sb % 2
            QX = [QA, QB][par]
            Y1B = [BK[0], BK[2]][par]
            Y2B = [BK[3], BK[1]][par]
            h0 = 8 * sb
            c.wait(TE, "VE", t_ab[1])
            c.wait(TE, "VE", t_qfree[par])
            for kk in range(8):
                TE.transpose(QX[0:3, kk * 128:(kk + 1) * 128],
                             splTI[:, h0 + kk:h0 + kk + 257:128], k_id[:])
            t_lt = c.inc(TE.drain(), "PE")
            c.wait(V, "PE", t_lt)
            V.tensor_copy(r3b8[:], QX[0:3, 0:1024])
            t_r3 = c.inc(V.drain(), "VE")
            c.wait(TE, "VE", t_r3)
            c.wait(TE, "VE", t_lwprev)
            for kk in range(8):
                RB = [BK[4], BK[5]][kk // 4]
                TE.matmul(RB[:, (kk % 4) * 128:(kk % 4 + 1) * 128], k_on3[:],
                          r3b8[:, kk * 128:(kk + 1) * 128], start=True, stop=True)
            t_rb = c.inc(TE.drain(), "PE")
            c.wait(V, "PE", t_rb)
            for kk in range(8):
                RB = [BK[4], BK[5]][kk // 4]
                V.tensor_scalar(ltw8[:, kk * 128:(kk + 1) * 128],
                                RB[:, (kk % 4) * 128:(kk % 4 + 1) * 128],
                                acT[:, h0 + kk:h0 + kk + 1], 0.0,
                                ALU.subtract, ALU.min)
            V.tensor_scalar_max(ltw8[:], ltw8[:], -80.0)
            t_lw = c.inc(V.drain(), "VE")
            t_lwprev = t_lw
            c.wait(S_, "VE", t_lw)
            S_.activation(ltw8[:], ltw8[:], AF.Exp)
            t_ex = c.inc(S_.drain(), "SE")
            c.wait(S_, "VE", t_prevz)
            c.wait(S_, "PE", t_y2pe)
            S_.copy(pvb[:], prevC[:, sb * 512:(sb + 1) * 512])
            t_pv = c.inc(S_.drain(), "SE")
            c.wait(V, "SE", t_ex)
            for kk in range(8):
                V.tensor_tensor(mtb8[:, kk * 128:(kk + 1) * 128],
                                gtm[:, g * 128:(g + 1) * 128],
                                ltw8[:, kk * 128:(kk + 1) * 128], ALU.mult)
            t_mtb = c.inc(V.drain(), "VE")
            c.wait(TE, "VE", t_mtb)
            c.wait2(TE, t_xcz)
            c.wait(TE, "VE", t_y1free[par])
            for kk in range(8):
                TE.matmul(Y1B[:, kk * 64:(kk + 1) * 64], mtb8[:, kk * 128:(kk + 1) * 128],
                          xch_z[:, (h0 + kk) * 64:(h0 + kk + 1) * 64],
                          start=True, stop=True)
            c.wait(TE, "SE", t_pv)
            c.wait(TE, "VE", t_y2free[par])
            TE.matmul(Y2B[:, 0:512], cz_c[:, g * 128:(g + 1) * 128], pvb[:],
                      start=True, stop=True)
            c.wait(TE, "VE", t_hsd)
            for q in range(4):
                TE.transpose(QX[:, q * 128:(q + 1) * 128],
                             hs_z[:, (4 * sb + q) * 128:(4 * sb + q + 1) * 128], k_id[:])
            t_pe2 = c.inc(TE.drain(), "PE")
            t_y2pe = t_pe2
            c.wait(V, "PE", t_pe2)
            c.wait(V, "SE", t_ab[0])
            if sb == 0:
                c.wait2(V, t_sgz)
                c.wait2(V, t_ssz)
            for kk in range(8):
                V.tensor_scalar(wrk8[:, kk * 64:(kk + 1) * 64], Y2B[:, kk * 64:(kk + 1) * 64],
                                expA[:, h0 + kk:h0 + kk + 1], None, ALU.mult)
            V.tensor_tensor(wrk8[:], wrk8[:], Y1B[:, 0:512], ALU.add)
            V.tensor_tensor(y_z[:, sb * 512:(sb + 1) * 512], wrk8[:], QX[:, 0:512], ALU.add)
            for kk in range(8):
                h = h0 + kk
                V.scalar_tensor_tensor(prevC[:, h * 64:(h + 1) * 64],
                                       prevC[:, h * 64:(h + 1) * 64],
                                       eazb[:, h:h + 1],
                                       s_z[:, h * 64:(h + 1) * 64], ALU.mult, ALU.add)
            tv = c.inc(V.drain(), "VE")
            t_qfree[par] = tv
            t_y1free[par] = tv
            t_y2free[par] = tv
        if debug and z == NZ - 1:
            tdv = c.inc(V.drain(), "VE")
            c.wait(SY, "VE", tdv)
            c.dma(SY, dbg["ylast"], y_z[:], "DBG")
            tdbg = c.dma(SY, dbg["gtm"], gtm[:], "DBG")
            c.wait2(V, tdbg)
        V.tensor_tensor(y_z[:], y_z[:], sg_z[:], ALU.mult)
        t_gt = c.inc(V.drain(), "VE")
        c.wait(S_, "VE", t_gt)
        t_red = [0, 0]
        for gi in range(8):
            sl = csc[:, (gi % 2) * 1024:(gi % 2 + 1) * 1024]
            c.wait(S_, "VE", t_red[gi % 2])
            S_.activation(sl, y_z[:, gi * 1024:(gi + 1) * 1024], AF.Square)
            t_sq = c.inc(S_.drain(), "SE")
            c.wait(V, "SE", t_sq)
            V.reduce_sum(sumg[:, gi:gi + 1], sl, axis=AXX)
            t_red[gi % 2] = c.inc(V.drain(), "VE")
        t_vr = c.inc(V.drain(), "VE")
        c.wait(S_, "VE", t_vr)
        S_.activation(rstd[:], sumg[:], AF.Ln, bias=k_sc[:, 0:1], scale=1.0 / 1024.0)
        c.fence(S_, "SE")
        S_.activation(rstd[:], rstd[:], AF.Exp, scale=-0.5)
        t_rs = c.inc(S_.drain(), "SE")
        c.wait(V, "SE", t_rs)
        for gi in range(8):
            V.tensor_scalar(y_z[:, gi * 1024:(gi + 1) * 1024],
                            y_z[:, gi * 1024:(gi + 1) * 1024],
                            rstd[:, gi:gi + 1], None, ALU.mult)
        t_gg = c.inc(V.drain(), "VE")
        c.wait(SY, "VE", t_gg)
        t_ggev = [0, 0]
        for p in range(64):
            par = p % 2
            QG = [QA, QB][par]
            c.wait(TE, "VE", t_gg if p < 2 else t_ggev[par])
            c.wait2(TE, t_zc[z])
            TE.transpose(QG[:, 0:128], y_z[:, p * 128:(p + 1) * 128], k_id[:])
            t_t = c.inc(TE.drain(), "PE")
            c.wait(V, "PE", t_t)
            V.tensor_scalar(ggTs[:, p * 128:(p + 1) * 128], QG[:, 0:128],
                            k_nw[:, p:p + 1], None, ALU.mult)
            t_ggev[par] = c.inc(V.drain(), "VE")
        t_gga = c.inc(V.drain(), "VE")
        c.wait(SY, "VE", t_gga)
        t_zc[z + 1] = c.dma(SY, d_ggt[:, :, z * 128:(z + 1) * 128].rearrange(
                                "k p t -> p k t"),
                            ggTs[:].rearrange("p (k t) -> p k t", k=64), "GGS")
        t_prevz = c.cnt["VE"]
        t_zend = c.cnt["VE"]
    t_dbg_pc = None
    if debug:
        c.wait(SY, "VE", c.cnt["VE"])
        c.dma(SY, dbg["yz"], prevC[:], "DBG")
        c.wait2(SY, t_zc[NZ])
        t_dbg_pc = c.dma(SY, dbg["ggt"][:, :, 0:1024], d_ggt[:, :, 0:1024], "DBG")
    if phases == 5:
        c.dma(SY, outT, xT, "OUT")
        c.final_dma_waits(SY)
        es.close()
        return nc

    c.wait2(SY, t_zc[NZ])
    c.wait(SY, "PE", c.cnt["PE"])
    c.wait(SY, "VE", c.cnt["VE"])
    c.wait(SY, "SE", c.cnt["SE"])
    t_ggl = None
    for q4 in range(4):
        t_ggl = c.dma(SY, ggT[:, q4 * 16384:(q4 + 1) * 16384].rearrange(
            "p (k t) -> p k t", k=16),
            d_ggt[q4 * 16:(q4 + 1) * 16, :, 0:1024].rearrange("k p t -> p k t"), "GGL")
    wdb = [wdb0, wdb1]
    c.wait2(GP, t_dbg_pc)
    c.wait(GP, "PE", c.cnt["PE"])
    c.wait(GP, "VE", c.cnt["VE"])
    c.wait(GP, "SE", c.cnt["SE"])
    t_wd = [None] * 32
    for ob in range(2):
        src = w_wo[:, ob * 4 * RUNP:(ob + 1) * 4 * RUNP].rearrange(
            "p (r q) -> p r q", q=RUNP)[:, :, 0:RUN]
        t_wd[ob] = c.dma(GP, wdb[ob % 2][:], src, f"WO{ob % 2}")
    t_oev = [0, 0, 0, 0]
    t_odma = [None, None]
    t_bufev = [0, 0]
    t_gxd = [None, None]
    for ob in range(32):
        c.wait2(TE, t_wd[ob])
        for tg in range(2):
            slot = (ob % 2) * 2 + tg
            c.wait(SY, "VE", t_bufev[tg])
            t_g1 = c.dma(SY, gft[:, tg * 512:(tg + 1) * 512],
                         d_gffn[ob * 128:(ob + 1) * 128, tg * 512:(tg + 1) * 512],
                         f"GT{tg}")
            t_g2 = c.dma(SY, xtt[:, tg * 512:(tg + 1) * 512],
                         xT[ob * 128:(ob + 1) * 128, tg * 512:(tg + 1) * 512],
                         f"XT{tg}")
            t_gxd[tg] = (t_g1, t_g2)
            pb_ = BK[slot]
            c.wait2(TE, t_ggl)
            c.wait(TE, "VE", t_oev[slot])
            for chk in range(64):
                TE.matmul(pb_[:, 0:512], wdb[ob % 2][:, chk * 128:(chk + 1) * 128],
                          ggT[:, chk * 1024 + tg * 512:chk * 1024 + (tg + 1) * 512],
                          start=(chk == 0), stop=(chk == 63))
            t_m = c.inc(TE.drain(), "PE")
            c.wait(V, "PE", t_m)
            c.wait2(V, t_gxd[tg][0])
            c.wait2(V, t_gxd[tg][1])
            c.wait2(V, t_odma[tg])
            V.tensor_tensor(otb[:, tg * 512:(tg + 1) * 512], pb_[:, 0:512],
                            gft[:, tg * 512:(tg + 1) * 512], ALU.mult)
            V.tensor_tensor(otb[:, tg * 512:(tg + 1) * 512],
                            otb[:, tg * 512:(tg + 1) * 512],
                            xtt[:, tg * 512:(tg + 1) * 512], ALU.add)
            t_oev[slot] = c.inc(V.drain(), "VE")
            t_bufev[tg] = t_oev[slot]
            c.wait(SY, "VE", t_oev[slot])
            t_odma[tg] = c.dma(SY, outT[ob * 128:(ob + 1) * 128, tg * 512:(tg + 1) * 512],
                               otb[:, tg * 512:(tg + 1) * 512], f"OS{tg}")
        if ob + 2 < 32:
            c.wait(GP, "PE", c.cnt["PE"])
            src = w_wo[:, (ob + 2) * 4 * RUNP:(ob + 3) * 4 * RUNP].rearrange(
                "p (r q) -> p r q", q=RUNP)[:, :, 0:RUN]
            t_wd[ob + 2] = c.dma(GP, wdb[ob % 2][:], src, f"WO{ob % 2}")
    c.final_dma_waits(SY)
    es.close()
    return nc


_CACHE = {}


def kernel(**inputs):
    st = _stage_weights(inputs)
    x = np.asarray(inputs["L0_attn_x"], np.float32)
    key = (_PHASES, _DEBUG)
    if key not in _CACHE:
        _CACHE[key] = build_nc(debug=_DEBUG, phases=_PHASES)
    nc = _CACHE[key]
    shared = {
        "w_ip": st["w_ip"], "w_fb": st["w_fb"], "w_wo": st["w_wo"],
        "w_al": st["w_al"], "w_bl": st["w_bl"], "w_rt": st["w_rt"],
        "c_convw": st["c_convw"], "c_convb": st["c_convb"],
        "c_dtb": st["c_dtb"], "c_alog": st["c_alog"], "c_dcol": st["c_dcol"],
        "c_drep": st["c_drep"],
        "c_nw": st["c_nw"], "c_bb": st["c_bb"], "c_id": st["c_id"],
        "c_tri": st["c_tri"], "c_on3": st["c_on3"], "c_on1": st["c_on1"],
        "c_em": st["c_em"], "c_sc": st["c_sc"],
    }
    in_maps = []
    for core in range(8):
        b, hf = core % 4, core // 4
        xh = np.zeros((XROWS, HID), np.float32)
        if hf == 1:
            xh[0:3] = x[b, 1021:1024]
        xh[3:1027] = x[b, hf * 1024:(hf + 1) * 1024]
        m = dict(shared)
        m["xh"] = xh
        m["xT"] = np.ascontiguousarray(x[b, hf * 1024:(hf + 1) * 1024].T)
        m["c_flag"] = np.array([[float(hf)]], np.float32)
        in_maps.append(m)
    res = run_bass_kernel_spmd(nc, in_maps, core_ids=list(range(8)))
    out = np.empty((4, 2048, HID), np.float32)
    for core in range(8):
        b, hf = core % 4, core // 4
        out[b, hf * 1024:(hf + 1) * 1024] = res.results[core]["outT"].T
    return out
